# version 12
# baseline (speedup 1.0000x reference)
import contextlib
import numpy as np
import ml_dtypes
import concourse.bass as bass
import concourse.mybir as mybir
from concourse.bass_utils import run_bass_kernel_spmd

F32 = mybir.dt.float32
BF16 = mybir.dt.bfloat16
AF = mybir.ActivationFunctionType
ALU = mybir.AluOpType
AX = mybir.AxisListType
bfd = ml_dtypes.bfloat16

TAPS = []
DEV_OUT = None
DEV_A2N = None
DEV_BATCHES = None
DEV_PHASES = None
DEV_CORES = None
N_CORES = 8

D = 1024
L = 2048
H = 16
DH = 64
NCH = 8
TC = 512
NTC = L // TC
NLT = L // 128
JW = 2176
EPS1 = 1e-5
EPS2 = 1e-6
FH = 8
FW = 512
NW = L // FW
FIN = 512
NH1 = 22
HID_PAD = 2816
ENGS = ("sync", "tensor", "vector", "scalar", "gpsimd")
ARENA_BYTES = 190 * 1024
FCONSTS = [EPS1 * EPS2, 1.0 + EPS2, EPS1, -0.5, 0.125, 1.0, 1.0 / D, 1e12, 0.5]


class Sem:
    __slots__ = ("h", "n", "name")

    def __init__(self, h, name):
        self.h = h
        self.n = 0
        self.name = name

    def bump(self, k):
        self.n += k
        return self.n


class KB:
    def __init__(self):
        self.nc = bass.Bass("TRN2", target_bir_lowering=False)
        self.sched = {e: [] for e in ENGS}
        self.es = contextlib.ExitStack()
        self.sems = {}
        self.tap_tensors = {}
        self.tap_sem = None
        self.n_ops = 0
        self.arena = None
        self.ar_off = 0
        self.ar_stack = []
        self.ar_views = {}
        self.dma_sems = set()
        self.free_list = [(0, ARENA_BYTES)]
        self.tags = {}
        self.live = []

    def sem(self, name):
        s = Sem(self.es.enter_context(self.nc.semaphore(name)), name)
        self.sems[name] = s
        return s

    def init_arena(self):
        t = self.es.enter_context(
            self.nc.sbuf_tensor("ARENA", [128, ARENA_BYTES], mybir.dt.uint8))
        self.ar_views = {F32: t.ap().bitcast(F32), BF16: t.ap().bitcast(BF16)}

    def push(self):
        self.ar_stack.append([])

    def pop(self):
        for off, nb in self.ar_stack.pop():
            self._free_range(off, nb)

    def _alloc_range(self, nbytes):
        nbytes = -(-nbytes // 32) * 32
        for i, (off, sz) in enumerate(self.free_list):
            if sz >= nbytes:
                if sz == nbytes:
                    self.free_list.pop(i)
                else:
                    self.free_list[i] = (off + nbytes, sz - nbytes)
                return off, nbytes
        raise AssertionError(
            f"SBUF arena overflow: need {nbytes}, free={self.free_list}")

    def _free_range(self, off, nbytes):
        self.live = [(o, n) for (o, n) in self.live if not (o == off and n == nbytes)]
        self.free_list.append((off, nbytes))
        self.free_list.sort()
        merged = []
        for o, s in self.free_list:
            if merged and merged[-1][0] + merged[-1][1] == o:
                merged[-1] = (merged[-1][0], merged[-1][1] + s)
            else:
                merged.append((o, s))
        self.free_list = [tuple(x) for x in merged]

    def buf(self, shape, dtype, tag=None):
        v, _ = self.buf2(shape, dtype, tag)
        return v

    def buf2(self, shape, dtype, tag=None):
        esz = 4 if dtype is F32 else 2
        free = 1
        for d in shape[1:]:
            free *= d
        off, nbytes = self._alloc_range(free * esz)
        for o2, n2 in self.live:
            assert off + nbytes <= o2 or o2 + n2 <= off, \
                f"ALLOC OVERLAP: [{off},{off+nbytes}) vs [{o2},{o2+n2})"
        self.live.append((off, nbytes))
        if tag is not None:
            self.tags.setdefault(tag, []).append((off, nbytes))
        elif self.ar_stack:
            self.ar_stack[-1].append((off, nbytes))
        v = self.ar_views[dtype][0:shape[0], off // esz: off // esz + free]
        if len(shape) == 3:
            v = v.rearrange("p (a b) -> p a b", a=shape[1], b=shape[2])
        elif len(shape) == 4:
            v = v.rearrange("p (a b c) -> p a b c", a=shape[1], b=shape[2], c=shape[3])
        return v, off // esz

    def free_tag(self, tag):
        for off, nb in self.tags.pop(tag, []):
            self._free_range(off, nb)

    def diag_ap(self, dtype, elem_off, npart, width):
        t = self.ar_views[dtype].tensor
        fs = ARENA_BYTES // (4 if dtype is F32 else 2)
        return bass.AP(t, elem_off + npart - 1, [[fs - 1, npart], [1, width]])

    def gsb(self, name, shape, dtype):
        return self.es.enter_context(self.nc.sbuf_tensor(name, shape, dtype)).ap()

    def op(self, eng, fn, waits=(), incs=(), is_dma=False):
        for s, v in waits:
            assert v <= s.n, f"wait on future inc of {s.name}: {v} > {s.n}"
        for s, k in incs:
            s.bump(k)
        ent = [tuple(waits), fn, list(incs), is_dma]
        self.sched[eng].append(ent)
        self.n_ops += 1
        return ent

    def dma(self, eng, sem, out, in_, waits=(), **kw):
        self.dma_sems.add(sem)
        self.op(eng, lambda e, o=out, i=in_, kw=kw: e.dma_start(out=o, in_=i, **kw),
                waits, [(sem, 16)], is_dma=True)
        return sem.n

    def barrier(self, tag=""):
        for e in ("tensor", "vector", "scalar", "gpsimd"):
            s = self.sems["e_" + e]
            for ent in reversed(self.sched[e]):
                if ent[3]:
                    continue
                if ent[1] is None:
                    break
                if not any(ss is s for ss, _ in ent[2]):
                    s.bump(1)
                    ent[2].append((s, 1))
                break
        snap = [(self.sems["e_" + e], self.sems["e_" + e].n) for e in ENGS]
        snap += [(s, s.n) for s in self.dma_sems]
        for e in ENGS:
            waits = [(s, v) for s, v in snap
                     if s is not self.sems["e_" + e] and v > 0]
            self.op(e, None, waits, [])

    def tap(self, name, ap, shape, dtype=F32):
        if name not in TAPS or name in self.tap_tensors:
            return
        t = self.nc.dram_tensor("tap_" + name, list(shape), dtype,
                                kind="ExternalOutput").ap()
        self.tap_tensors[name] = t
        waits = [(self.sems["e_" + e], self.sems["e_" + e].n) for e in ENGS
                 if self.sems["e_" + e].n > 0]
        self.dma("sync", self.tap_sem, t, ap, waits=waits)

    def build(self):
        sched = self.sched

        def run(e, name):
            for waits, fn, incs, _dma in sched[name]:
                for s, v in waits:
                    e.wait_ge(s.h, v)
                ins = fn(e) if fn is not None else None
                if incs:
                    if ins is None:
                        ins = e.nop()
                    for s, k in incs:
                        ins.then_inc(s.h, k)

        with self.nc.Block() as block:
            block.sync(lambda e: run(e, "sync"))
            block.tensor(lambda e: run(e, "tensor"))
            block.vector(lambda e: run(e, "vector"))
            block.scalar(lambda e: run(e, "scalar"))
            block.gpsimd(lambda e: run(e, "gpsimd"))



def host_constants():
    c = {}
    c["ident"] = np.eye(128, dtype=np.float32)
    c["ones1"] = np.ones((1, 128), np.float32)
    c["fpool"] = np.tile(np.asarray(FCONSTS, np.float32)[None, :], (128, 1))
    half = np.arange(0, DH, 2, dtype=np.float64)
    inv_freq = 1.0 / 10000.0 ** (half / DH)
    t = np.arange(L, dtype=np.float64)
    fr32 = t[:, None] * inv_freq[None, :]
    base = (half + 0.4 * DH) / (1.4 * DH)
    power = (t - L // 2) / float(FW // 2)
    sc32 = base[None, :] ** power[:, None]
    cos64 = np.cos(np.concatenate([fr32, fr32], 1))
    sin64 = np.sin(np.concatenate([fr32, fr32], 1))
    sc64 = np.concatenate([sc32, sc32], 1)
    QKS = 8.0
    c["rotqc"] = np.tile((cos64 * sc64).T * QKS, (2, 1)).astype(np.float32)
    c["rotqs"] = np.tile((sin64 * sc64).T * QKS, (2, 1)).astype(np.float32)
    c["rotkc"] = np.tile((cos64 / sc64).T, (2, 1)).astype(np.float32)
    c["rotks"] = np.tile((sin64 / sc64).T, (2, 1)).astype(np.float32)
    rotm = np.zeros((128, 128), np.float32)
    for blk in range(2):
        o = 64 * blk
        for dh in range(32):
            rotm[o + dh + 32, o + dh] = -1.0
            rotm[o + dh, o + dh + 32] = 1.0
    c["rotm"] = rotm
    masks = np.zeros((8, 128, 1024), np.float32)
    for w0 in range(2):
        for lt in range(4):
            l = lt * 128 + np.arange(128)[:, None]
            jj = np.arange(1024)[None, :] - 512
            bad = (l < jj) | (l > jj + FW)
            if w0:
                bad = bad | (jj < 0)
            masks[w0 * 4 + lt][bad] = -1e30
    c["masks"] = masks
    bd8 = np.zeros((4, 128, 8), np.float32)
    oneh = np.zeros((4, 8, 128), np.float32)
    for cc in range(4):
        for p in range(128):
            j = 2 * cc + p // 64
            bd8[cc, p, j] = 1.0
            oneh[cc, j, p] = 1.0
    c["bd8"] = bd8
    c["oneh"] = oneh
    return c



def build_program():
    kb = KB()
    nc = kb.nc

    def dinp(name, shape, dtype=F32):
        return nc.dram_tensor(name, list(shape), dtype, kind="ExternalInput").ap()

    xT = dinp("xT", [2, NCH, 128, L])
    Wq = dinp("Wq", [D, D]); Wk = dinp("Wk", [D, D]); Wv = dinp("Wv", [D, D])
    Wr = dinp("Wr", [D, D]); Wo = dinp("Wo", [D, D])
    gamma8 = dinp("gamma8", [NCH, 128]); beta8 = dinp("beta8", [NCH, 128])
    bo8 = dinp("bo8", [NCH, 128])
    ub8 = dinp("ub8", [NCH, 128]); vb8 = dinp("vb8", [NCH, 128])
    rel = dinp("rel", [4095, D])
    fw_qkv = dinp("fw_qkv", [2, D, 3 * FIN])
    fw_wo = dinp("fw_wo", [2, FIN, D])
    fw_w1 = dinp("fw_w1", [2, D, 2, HID_PAD])
    fw_b1 = dinp("fw_b1", [2, 2, NH1, 128])
    fw_w2 = dinp("fw_w2", [2, HID_PAD, D])
    fw_b2 = dinp("fw_b2", [2, NCH, 128])
    f_qs = dinp("f_qs", [2, 128]); f_ks = dinp("f_ks", [2, 128])
    f_lng = dinp("f_lng", [2, NCH, 128]); f_lnb = dinp("f_lnb", [2, NCH, 128])
    c_ident = dinp("c_ident", [128, 128])
    c_ones1 = dinp("c_ones1", [1, 128])
    c_rotqc = dinp("c_rotqc", [128, L]); c_rotqs = dinp("c_rotqs", [128, L])
    c_rotkc = dinp("c_rotkc", [128, L]); c_rotks = dinp("c_rotks", [128, L])
    c_rotm = dinp("c_rotm", [128, 128])
    c_masks = dinp("c_masks", [8, 128, 1024])
    c_fpool = dinp("c_fpool", [128, len(FCONSTS)])
    c_bd8 = dinp("c_bd8", [4, 128, 8])
    c_oneh = dinp("c_oneh", [4, 8, 128])

    outT = nc.dram_tensor("outT", [2, NCH, 128, L], F32, kind="ExternalOutput").ap()
    rt_scr = nc.dram_tensor("rt_scr", [D, 4352], BF16, kind="Internal").ap()
    x1_park = nc.dram_tensor("x1_park", [NCH, 128, L], F32, kind="Internal").ap()

    for e in ENGS:
        kb.sem("e_" + e)
    sS = kb.sems["e_sync"]
    sT = kb.sems["e_tensor"]
    sV = kb.sems["e_vector"]
    sC = kb.sems["e_scalar"]
    sG = kb.sems["e_gpsimd"]
    sW = kb.sem("wload")
    sX = kb.sem("xload")
    sD = kb.sem("gdiag")
    sO = kb.sem("outs")
    sWs = [kb.sem("wslot0"), kb.sem("wslot1")]
    sRTs = [kb.sem("rtslot0"), kb.sem("rtslot1")]
    sDs = [kb.sem("dslot0"), kb.sem("dslot1"), kb.sem("dslot2")]
    kb.tap_sem = kb.sem("taps")

    kb.init_arena()
    ident = kb.gsb("ident", [128, 128], BF16)
    ones1b = kb.gsb("ones1b", [1, 128], BF16)
    ones1f = kb.gsb("ones1f", [1, 128], F32)
    onescf = kb.gsb("onescf", [128, 1], F32)
    onescb = kb.gsb("onescb", [128, 1], BF16)
    fpool = kb.gsb("fpool", [128, len(FCONSTS)], F32)
    gam_sb = kb.gsb("gam_sb", [128, NCH], F32)
    bet_sb = kb.gsb("bet_sb", [128, NCH], F32)
    bo_sb = kb.gsb("bo_sb", [128, NCH], F32)
    ub_sb = kb.gsb("ub_sb", [128, NCH], F32)
    vb_sb = kb.gsb("vb_sb", [128, NCH], F32)
    dvb_sb = kb.gsb("dvb_sb", [128, NCH], F32)
    row_f = kb.gsb("row_f", [1, 4, TC], F32)

    psA_t = kb.es.enter_context(nc.psum_tensor("psA", [128, 4096], F32))
    psA = psA_t.ap()
    psAb = psA.bitcast(BF16)

    def bank(i, nf32=512):
        return psA[:, 512 * i: 512 * i + nf32]

    def bankb(i, nbf=1024):
        return psAb[:, 1024 * i: 1024 * i + nbf]

    kb.dma("gpsimd", sW, ident[:], c_ident[:])
    kb.dma("gpsimd", sW, ones1b[0:1, :], c_ones1[:])
    kb.dma("sync", sW, ones1f[0:1, :], c_ones1[:])
    kb.dma("sync", sW, onescf[:, 0:1], c_ones1.rearrange("o p -> p o"))
    kb.dma("gpsimd", sW, onescb[:, 0:1], c_ones1.rearrange("o p -> p o"))
    kb.dma("sync", sW, fpool[:], c_fpool[:])
    for j, v in enumerate(FCONSTS):
        nc.const_aps.register(F32, float(np.float32(v)), fpool[:, j:j + 1])
        nc.const_aps.register(F32, float(v), fpool[:, j:j + 1])

    for i in range(NCH):
        kb.dma("sync", sW, gam_sb[:, i:i + 1], gamma8[i:i + 1, :].rearrange("o p -> p o"))
        kb.dma("sync", sW, bet_sb[:, i:i + 1], beta8[i:i + 1, :].rearrange("o p -> p o"))
        kb.dma("sync", sW, bo_sb[:, i:i + 1], bo8[i:i + 1, :].rearrange("o p -> p o"))
        kb.dma("sync", sW, ub_sb[:, i:i + 1], ub8[i:i + 1, :].rearrange("o p -> p o"))
        kb.dma("sync", sW, vb_sb[:, i:i + 1], vb8[i:i + 1, :].rearrange("o p -> p o"))
    kb.op("vector",
          lambda e: e.tensor_tensor(dvb_sb[:], vb_sb[:], ub_sb[:], ALU.subtract),
          waits=[(sW, sW.n)], incs=[(sV, 1)])
    kb.barrier("consts")

    def _recip_lp(e, out, inp):
        with nc.allow_low_precision(reason="softmax normalizer cast to bf16 (same as prior ScalarE-copy path)"):
            return e.reciprocal(out, inp)

    def ph_on(name):
        return (DEV_PHASES is None or DEV_PHASES == {"ALL"}
                or name in DEV_PHASES)

    if ph_on("R"):
        kb.push()
        relT = kb.buf([128, NCH, 4096], BF16)
        rel_sb = kb.buf([128, 2, D], BF16)
        wr_sb = kb.buf([128, NCH, 1024], BF16)
        rt_out = kb.buf([128, 2, 512], BF16)
        sRL = kb.sem("r_rel")

        for ch in range(NCH):
            kb.dma("gpsimd", sW, wr_sb[:, ch, :], Wr[128 * ch:128 * ch + 128, :])
        wr_done = sW.n

        slot_guard = {0: 0, 1: 0}
        for jt in range(32):
            rows = 128 if jt < 31 else 127
            slot = jt % 2
            kb.dma("gpsimd", sRL, rel_sb[0:rows, slot, :],
                   rel[128 * jt:128 * jt + rows, :],
                   waits=[(sT, slot_guard[slot])])
            ld = sRL.n
            for ch in range(NCH):
                b = ch % 2
                kb.op("tensor",
                      lambda e, rows=rows, slot=slot, ch=ch, b=b:
                      e.transpose(bankb(b)[0:128, 0:rows],
                                  rel_sb[0:rows, slot, 128 * ch:128 * ch + 128],
                                  ident[0:rows, 0:rows]),
                      waits=[(sRL, ld), (sV, sV.n)],
                      incs=[(sT, 1)])
                kb.op("vector",
                      lambda e, rows=rows, jt=jt, ch=ch, b=b:
                      e.tensor_copy(relT[:, ch, 128 * jt:128 * jt + rows],
                                    bankb(b)[0:128, 0:rows]),
                      waits=[(sT, sT.n)], incs=[(sV, 1)])
            slot_guard[slot] = sT.n
        kb.barrier("R-transposed")

        rt_guard = {0: 0, 1: 0}
        for do in range(NCH):
            for jc in range(8):
                b = jc % 2
                for ch in range(NCH):
                    kb.op("tensor",
                          lambda e, do=do, jc=jc, ch=ch, b=b:
                          e.matmul(bank(b), wr_sb[:, ch, 128 * do:128 * do + 128],
                                   relT[:, ch, 512 * jc:512 * jc + 512],
                                   start=(ch == 0), stop=(ch == NCH - 1)),
                          waits=[(sW, wr_done), (sC, sC.n)] if ch == 0 else [],
                          incs=[(sT, 1)] if ch == NCH - 1 else [])
                kb.op("scalar",
                      lambda e, b=b: e.copy(rt_out[:, b, :], bank(b)),
                      waits=[(sT, sT.n), (sS, rt_guard[b])], incs=[(sC, 1)])
                rt_guard[b] = kb.dma(
                    "sync", sS,
                    rt_scr[128 * do:128 * do + 128, 512 * jc:512 * jc + 512],
                    rt_out[:, b, :], waits=[(sC, sC.n)])
        kb.barrier("R-done")
        kb.pop()


    def ln_stats(src_tiles, eps_bias, eps_scale, sq_scr, mrow, rrow, ready=(),
                 ones_src=None):
        if ones_src is None:
            ones_src = onescf
        tiles = [src_tiles(ch) for ch in range(NCH)]
        src_tiles = lambda ch: tiles[ch]
        for ch in range(NCH):
            w = list(ready) if ch == 0 else []
            if ch == 0:
                w += [(sV, sV.n)]
            kb.op("scalar",
                  lambda e, t=tiles[ch]: e.activation(sq_scr[:], t, AF.Square),
                  waits=(list(ready) if ch == 0 else []) + [(sT, sT.n)],
                  incs=[(sC, 1)])
            kb.op("tensor",
                  lambda e, t=tiles[ch], ch=ch: e.matmul(bank(6)[0:1, :], ones_src[:, 0:1], t,
                                            start=(ch == 0), stop=(ch == NCH - 1)),
                  waits=w, incs=[(sT, 1)])
            kb.op("tensor",
                  lambda e, ch=ch: e.matmul(bank(7)[0:1, :], onescf[:, 0:1], sq_scr[:],
                                            start=(ch == 0), stop=(ch == NCH - 1)),
                  waits=[(sC, sC.n)], incs=[(sT, 1)])
        kb.op("vector",
              lambda e: e.tensor_scalar(mrow, bank(6)[0:1, :], 1.0 / D, None, ALU.mult),
              waits=[(sT, sT.n)], incs=[(sV, 1)])
        kb.op("vector",
              lambda e: e.tensor_tensor(rrow, mrow, mrow, ALU.mult),
              waits=[(sV, sV.n)], incs=[(sV, 1)])
        kb.op("vector",
              lambda e: e.scalar_tensor_tensor(rrow, bank(7)[0:1, :], 1.0 / D, rrow,
                                               ALU.mult, ALU.subtract),
              waits=[(sV, sV.n)], incs=[(sV, 1)])
        kb.op("scalar",
              lambda e: e.activation(rrow, rrow, AF.Ln, bias=eps_bias, scale=eps_scale),
              waits=[(sV, sV.n)], incs=[(sC, 1)])
        kb.op("scalar",
              lambda e: e.activation(rrow, rrow, AF.Exp, scale=-0.5),
              waits=[(sC, sC.n)], incs=[(sC, 1)])

    def replicate2(mrow, rrow, mr_sb):
        kb.op("tensor",
              lambda e: e.matmul(bank(4), ones1f[0:1, :], mrow, start=True, stop=True),
              waits=[(sC, sC.n), (sV, sV.n)], incs=[(sT, 1)])
        kb.op("tensor",
              lambda e: e.matmul(bank(5), ones1f[0:1, :], rrow, start=True, stop=True),
              incs=[(sT, 1)])
        t = sT.n
        kb.op("scalar", lambda e: e.copy(mr_sb[:, 0, :], bank(4)),
              waits=[(sT, t), (sV, sV.n)], incs=[(sC, 1)])
        kb.op("scalar", lambda e: e.copy(mr_sb[:, 1, :], bank(5)),
              waits=[(sT, t)], incs=[(sC, 1)])
        return sC.n


    def phase_A1(b, Q1_sb, K_sb, V_tm):
        kb.push()
        xn_sb = kb.buf([128, NCH, L], BF16)
        x_sb = kb.buf([128, NCH, TC], F32)
        sq_scr = kb.buf([128, TC], F32)
        mr_sb = kb.buf([128, 2, TC], BF16)
        w_sb = kb.buf([128, 2, NCH, 128], BF16)
        vtmp = kb.buf([128, TC], BF16)

        for h in range(H):
            kb.op("vector",
                  lambda e, h=h: e.memset(V_tm[:, :, 65 * h + 64: 65 * h + 65], 1.0),
                  incs=[(sV, 1)])

        xg = {"V": 0, "T": 0, "C": 0}
        for tc in range(NTC):
            tsl = slice(TC * tc, TC * tc + TC)
            for ch in range(NCH):
                kb.dma("sync", sX, x_sb[:, ch, :], xT[b, ch, :, tsl],
                       waits=[(sV, xg["V"]), (sT, xg["T"]), (sC, xg["C"])])
            xw = sX.n
            mrow = row_f[0:1, 0, :]
            rrow = row_f[0:1, 1, :]
            ln_stats(lambda ch: x_sb[:, ch, :], EPS1 * EPS2, 1.0 + EPS2,
                     sq_scr, mrow, rrow, ready=[(sX, xw)])
            xg["T"] = sT.n
            xg["C"] = sC.n
            mrd = replicate2(mrow, rrow, mr_sb)
            for ch in range(NCH):
                kb.op("vector",
                      lambda e, ch=ch, tsl=tsl:
                      e.tensor_tensor(xn_sb[:, ch, tsl], x_sb[:, ch, :],
                                      mr_sb[:, 0, :], ALU.subtract),
                      waits=[(sC, mrd)], incs=[(sV, 1)])
                kb.op("vector",
                      lambda e, ch=ch, tsl=tsl:
                      e.tensor_tensor(xn_sb[:, ch, tsl], xn_sb[:, ch, tsl],
                                      mr_sb[:, 1, :], ALU.mult),
                      waits=[(sV, sV.n)], incs=[(sV, 1)])
                kb.op("vector",
                      lambda e, ch=ch, tsl=tsl:
                      e.tensor_scalar(xn_sb[:, ch, tsl], xn_sb[:, ch, tsl],
                                      gam_sb[:, ch:ch + 1], bet_sb[:, ch:ch + 1],
                                      ALU.mult, ALU.add),
                      waits=[(sV, sV.n)], incs=[(sV, 1)])
            xg["V"] = sV.n
            if tc == 0:
                kb.tap("x0", x_sb, [128, NCH, TC], F32)
                kb.tap("mr0", mr_sb, [128, 2, TC], BF16)
        xn_done_V = sV.n
        kb.tap("xn", xn_sb, [128, NCH, L], BF16)

        projs = [("q", Wq), ("k", Wk), ("v", Wv)]
        wslot_guard = {0: 0, 1: 0}
        for pi, (pname, Wap) in enumerate(projs):
            for dc in range(NCH):
                it = pi * NCH + dc
                slot = it % 2
                pb = 4 * (it % 2)
                for ch in range(NCH):
                    kb.dma("gpsimd", sWs[slot], w_sb[:, slot, ch, :],
                           Wap[128 * ch:128 * ch + 128, 128 * dc:128 * dc + 128],
                           waits=[(sT, wslot_guard[slot])])
                first_w = [(sWs[slot], sWs[slot].n), (sV, sV.n), (sC, sC.n)]
                for ch in range(NCH):
                    for tc in range(NTC):
                        tsl = slice(TC * tc, TC * tc + TC)
                        kb.op("tensor",
                              lambda e, ch=ch, tc=tc, slot=slot, pb=pb, tsl=tsl:
                              e.matmul(bank(pb + tc), w_sb[:, slot, ch, :],
                                       xn_sb[:, ch, tsl],
                                       start=(ch == 0), stop=(ch == NCH - 1)),
                              waits=first_w if (ch == 0 and tc == 0) else [],
                              incs=[(sT, 1)] if (ch == NCH - 1 and tc == NTC - 1) else [])
                wslot_guard[slot] = sT.n
                mmd = sT.n
                for tc in range(NTC):
                    tsl = slice(TC * tc, TC * tc + TC)
                    if pname == "q":
                        kb.op("scalar",
                              lambda e, dc=dc, tc=tc, pb=pb, tsl=tsl:
                              e.activation(Q1_sb[:, dc, tsl], bank(pb + tc),
                                           AF.Identity, bias=ub_sb[:, dc:dc + 1]),
                              waits=[(sT, mmd)], incs=[(sC, 1)])
                    elif pname == "k":
                        kb.op("scalar",
                              lambda e, dc=dc, tc=tc, pb=pb, tsl=tsl:
                              e.copy(K_sb[:, dc, tsl], bank(pb + tc)),
                              waits=[(sT, mmd)], incs=[(sC, 1)])
                    else:
                        kb.op("scalar",
                              lambda e, tc=tc, pb=pb:
                              e.copy(vtmp[:], bank(pb + tc)),
                              waits=[(sT, mmd), (sV, sV.n)], incs=[(sC, 1)])
                        evd = sC.n
                        ob = 4 - pb
                        for hh in range(2):
                            hD = 2 * dc + hh
                            for blk in range(4):
                                cblk = tc * 4 + blk
                                tbk = ob + (blk % 2)
                                kb.op("tensor",
                                      lambda e, hh=hh, blk=blk, tbk=tbk:
                                      e.transpose(bankb(tbk)[0:128, 0:64],
                                                  vtmp[64 * hh:64 * hh + 64,
                                                       128 * blk:128 * blk + 128],
                                                  ident[64 * hh:64 * hh + 64,
                                                        64 * hh:64 * hh + 64]),
                                      waits=[(sC, evd), (sV, sV.n)],
                                      incs=[(sT, 1)])
                                kb.op("vector",
                                      lambda e, hD=hD, cblk=cblk, tbk=tbk:
                                      e.tensor_copy(V_tm[:, cblk, 65 * hD:65 * hD + 64],
                                                    bankb(tbk)[0:128, 0:64]),
                                      waits=[(sT, sT.n)],
                                      incs=[(sV, 1)])
        kb.tap("q1", Q1_sb, [128, NCH, L], BF16)
        kb.tap("k", K_sb, [128, NCH, L], BF16)
        kb.tap("vtm", V_tm, [128, NLT, H * 65], BF16)
        kb.barrier("A1-done")
        kb.pop()


    def phase_A2(b, Q1_sb, K_sb, V_tm, Ofm):
        kb.push()
        NS_BD, NS_S, NS_Z, NS_P = 2, 3, 3, 2
        rt_sb = kb.buf([128, 2, 4096], BF16)
        BDsb, bd_off = kb.buf2([128, NS_BD, 2176], BF16)
        Ssb = kb.buf([128, NS_S, L], BF16)
        Zsb = kb.buf([128, NS_Z, L], BF16)
        PT = kb.buf([128, NS_P, L], BF16)
        q2t = kb.buf([128, 2, 128], BF16)
        negmax = kb.buf([128, 4], F32)
        rr = kb.buf([1, 2, 128], F32)
        rp = kb.buf([64, 2, 128], BF16)
        sRT = kb.sems.get("rtld") or kb.sem("rtld")

        PVo = psA[0:65, 3072:3200]
        PVrow = psA[64:65, 3072:3200]
        RPp = psA[0:64, 3200:3328]

        gd = {
            "bd_ev": [0, 0],
            "ac_ev": [0, 0],
            "bdsb_D": [0] * NS_BD,
            "ssb_V": [0] * NS_S, "ssb_D": [0] * NS_S,
            "zsb_T": [0] * NS_Z,
            "zt_C": [0, 0],
            "pt_T": [0] * NS_P,
            "b6_V": 0, "b6_C": 0, "rp_V": 0,
            "rt_T": [0, 0],
            "q2_T": [0, 0],
            "diag_D": [0] * 999,
            "acev_C": [0] * 999,
            "bdev_V": [0] * 999,
            "sub_V": [0] * 999,
            "exp_C": [0] * 999, "exp1_C": [0] * 999,
        }
        def allw():
            return [(sT, sT.n), (sV, sV.n), (sC, sC.n), (sD, sD.n)]

        iters = []
        for hp in range(8):
            for hh in range(2):
                for lt in range(NLT):
                    iters.append((hp, hh, lt))
        if DEV_A2N:
            iters = iters[:DEV_A2N]
        NIT = len(iters)
        rtld_n = {}

        def S0(i):
            hp, hh, lt = iters[i]
            if hh == 0 and lt == 0:
                for target in ([hp, hp + 1] if hp == 0 else [hp + 1]):
                    if target < 8:
                        slot = target % 2
                        kb.dma("sync", sRTs[slot], rt_sb[:, slot, :],
                               rt_scr[128 * target:128 * target + 128, 0:4096],
                               waits=[(sT, sT.n)])
                        rtld_n[target] = (sRTs[slot], sRTs[slot].n)
            pi = i % 2
            hb = 64 * hh
            hsl = slice(hb, hb + 64)
            lsl = slice(128 * lt, 128 * lt + 128)
            j0 = 1920 - 128 * lt
            kb.op("vector",
                  lambda e, hsl=hsl, pi=pi, hp=hp, lsl=lsl:
                  e.tensor_scalar(q2t[hsl, pi, :], Q1_sb[hsl, hp, lsl],
                                  dvb_sb[hsl, hp:hp + 1], None, ALU.add),
                  waits=[(sT, gd["q2_T"][pi])], incs=[(sV, 1)])
            q2v = sV.n
            sb = i % NS_BD
            for jc in range(5):
                n = 512 if jc < 4 else 128
                bki = jc % 2
                w = [(sV, gd["bd_ev"][bki])]
                if jc == 0:
                    w += [(sV, q2v), rtld_n[hp]]
                kb.op("tensor",
                      lambda e, jc=jc, n=n, bki=bki, hsl=hsl, pi=pi,
                      slot=hp % 2, j0=j0:
                      e.matmul(psA[:, 512 * bki:512 * bki + n],
                               q2t[hsl, pi, :],
                               rt_sb[hsl, slot, j0 + 512 * jc:j0 + 512 * jc + n],
                               start=True, stop=True),
                      waits=w, incs=[(sT, 1)])
                wv = [(sT, sT.n)]
                if jc == 0 and gd["bdsb_D"][sb] != 0:
                    wv.append(gd["bdsb_D"][sb])
                kb.op("vector",
                      lambda e, jc=jc, n=n, bki=bki, sb=sb:
                      e.tensor_copy(BDsb[:, sb, 512 * jc:512 * jc + n],
                                    psA[:, 512 * bki:512 * bki + n]),
                      waits=wv, incs=[(sV, 1)])
                gd["bd_ev"][bki] = sV.n
            gd["q2_T"][pi] = sT.n
            gd["bdev_V"][i] = sV.n
            ss = i % NS_S
            for c in range(4):
                bki = c % 2
                w = [(sC, gd["ac_ev"][bki])]
                kb.op("tensor",
                      lambda e, c=c, bki=bki, hsl=hsl, hp=hp, lsl=lsl:
                      e.matmul(psA[:, 1024 + 512 * bki:1536 + 512 * bki],
                               Q1_sb[hsl, hp, lsl],
                               K_sb[hsl, hp, 512 * c:512 * c + 512],
                               start=True, stop=True),
                      waits=w, incs=[(sT, 1)])
                wv = [(sT, sT.n)]
                if c == 0:
                    wv += [(sV, gd["ssb_V"][ss])]
                    if gd["ssb_D"][ss] != 0:
                        wv.append(gd["ssb_D"][ss])
                kb.op("scalar",
                      lambda e, c=c, bki=bki, ss=ss:
                      e.copy(Ssb[:, ss, 512 * c:512 * c + 512],
                             psA[:, 1024 + 512 * bki:1536 + 512 * bki]),
                      waits=wv, incs=[(sC, 1)])
                gd["ac_ev"][bki] = sC.n
            gd["acev_C"][i] = sC.n

        def S1(i):
            sb = i % NS_BD
            ss = i % NS_S
            dsem = sDs[i % 3]
            kb.op("gpsimd",
                  lambda e, ss=ss, sb=sb:
                  e.dma_start(out=Ssb[:, ss, :],
                              in_=kb.diag_ap(BF16, bd_off + sb * 2176, 128, L),
                              accum_op=ALU.add),
                  waits=[(sC, gd["acev_C"][i]), (sV, gd["bdev_V"][i])],
                  incs=[(dsem, 16)], is_dma=True)
            kb.dma_sems.add(dsem)
            gd["bdsb_D"][sb] = (dsem, dsem.n)
            gd["diag_D"][i] = (dsem, dsem.n)

        def S2(i):
            ss = i % NS_S
            sz = i % NS_Z
            nm = i % 4
            kb.op("vector",
                  lambda e, ss=ss, nm=nm:
                  e.tensor_reduce(negmax[:, nm:nm + 1], Ssb[:, ss, :],
                                  AX.X, ALU.max, negate=True),
                  waits=[gd["diag_D"][i]], incs=[(sV, 1)])

            kb.op("vector",
                  lambda e, ss=ss, sz=sz, nm=nm:
                  e.tensor_scalar(Zsb[:, sz, :], Ssb[:, ss, :],
                                  negmax[:, nm:nm + 1], None, ALU.add),
                  waits=[(sV, sV.n), (sT, gd["zsb_T"][sz])],
                  incs=[(sV, 1)])
            gd["ssb_V"][ss] = sV.n
            gd["ssb_D"][ss] = gd["diag_D"][i]
            gd["sub_V"][i] = sV.n

        def S3(i):
            sz = i % NS_Z
            sp = i % NS_P
            for k in range(16):
                tb = 4 + k // 8
                w = []
                if k == 0:
                    w = [(sV, gd["sub_V"][i]), (sC, gd["zt_C"][0])]
                elif k == 8:
                    w = [(sC, gd["zt_C"][1])]
                kb.op("tensor",
                      lambda e, k=k, tb=tb, sz=sz:
                      e.transpose(bankb(tb)[:, 128 * (k % 8):128 * (k % 8) + 128],
                                  Zsb[:, sz, 128 * k:128 * k + 128],
                                  ident[:]),
                      waits=w, incs=[(sT, 1)] if k in (7, 15) else [])
                if k == 7:
                    t1 = sT.n
            t2 = sT.n
            gd["zsb_T"][sz] = sT.n
            kb.op("scalar",
                  lambda e, sp=sp:
                  e.activation(PT[:, sp, 0:1024], bankb(4), AF.Exp, scale=0.125),
                  waits=[(sT, t1), (sT, gd["pt_T"][sp])], incs=[(sC, 1)])
            gd["zt_C"][0] = sC.n
            gd["exp1_C"][i] = sC.n
            kb.op("scalar",
                  lambda e, sp=sp:
                  e.activation(PT[:, sp, 1024:2048], bankb(5), AF.Exp, scale=0.125),
                  waits=[(sT, t2)], incs=[(sC, 1)])
            gd["zt_C"][1] = sC.n
            gd["exp_C"][i] = sC.n

        def S4(i):
            hp, hh, lt = iters[i]
            h = 2 * hp + hh
            hb = 64 * hh
            hsl = slice(hb, hb + 64)
            lsl = slice(128 * lt, 128 * lt + 128)
            sp = i % NS_P
            pi = i % 2
            for mc in range(16):
                w = []
                if mc == 0:
                    w = [(sC, gd["exp1_C"][i]), (sV, gd["b6_V"]), (sC, gd["b6_C"])]
                elif mc == 8:
                    w = [(sC, gd["exp_C"][i])]
                kb.op("tensor",
                      lambda e, mc=mc, h=h, sp=sp:
                      e.matmul(PVo, V_tm[:, mc, 65 * h:65 * h + 65],
                               PT[:, sp, 128 * mc:128 * mc + 128],
                               start=(mc == 0), stop=(mc == 15)),
                      waits=w, incs=[(sT, 1)] if mc == 15 else [])
            gd["pt_T"][sp] = sT.n
            pvd = sT.n
            kb.op("vector",
                  lambda e, pi=pi: e.reciprocal(rr[0:1, pi, :], PVrow),
                  waits=[(sT, pvd)], incs=[(sV, 1)])
            kb.op("tensor",
                  lambda e, pi=pi:
                  e.matmul(RPp, ones1f[0:1, 0:64], rr[0:1, pi, :],
                           start=True, stop=True),
                  waits=[(sV, sV.n), (sC, gd["b6_C"])], incs=[(sT, 1)])
            kb.op("scalar",
                  lambda e, pi=pi: e.copy(rp[:, pi, :], RPp),
                  waits=[(sT, sT.n)], incs=[(sC, 1)])
            gd["b6_C"] = sC.n
            kb.op("vector",
                  lambda e, hsl=hsl, hp=hp, lsl=lsl, pi=pi:
                  e.tensor_tensor(Ofm[hsl, hp, lsl], psA[0:64, 3072:3200],
                                  rp[:, pi, :], ALU.mult),
                  waits=[(sC, sC.n)], incs=[(sV, 1)])
            gd["b6_V"] = sV.n

        for cyc in range(NIT + 4):
            if cyc < NIT:
                S0(cyc)
            if 1 <= cyc < NIT + 1:
                S1(cyc - 1)
            if 2 <= cyc < NIT + 2:
                S2(cyc - 2)
            if 3 <= cyc < NIT + 3:
                S3(cyc - 3)
            if 4 <= cyc < NIT + 4:
                S4(cyc - 4)

        kb.tap("ofm", Ofm, [128, NCH, L], BF16)
        kb.barrier("A2-done")
        if b == 0:
            li = NIT - 1
            kb.tap("sdump", Ssb[:, li % NS_S, :], [128, L], BF16)
            kb.tap("bddump", BDsb[:, li % NS_BD, :], [128, 2304], BF16)
            kb.tap("ptdump", PT[:, li % NS_P, :], [128, L], BF16)
            kb.tap("rtdump", rt_sb[:, 0, :], [128, 4096], BF16)
            kb.tap("q2dump", q2t[:, li % 2, :], [128, 128], BF16)
        kb.barrier("A2-dumps")
        kb.pop()


    def phase_A3(b, Ofm, z_sb):
        kb.push()
        wo_sb = kb.buf([128, NCH, D], BF16)
        a_sb = kb.buf([128, NCH, TC], BF16)
        x_sb = kb.buf([128, NCH, TC], F32)
        x1c = kb.buf([128, NCH, TC], F32)
        sq_scr = kb.buf([128, TC], F32)
        mr_sb = kb.buf([128, 2, TC], BF16)
        mr2_sb = kb.buf([128, 2, TC], BF16)

        for ch in range(NCH):
            kb.dma("gpsimd", sW, wo_sb[:, ch, :], Wo[128 * ch:128 * ch + 128, :])
            kb.op("vector",
                  lambda e, ch=ch: e.memset(z_sb[:, ch, 0:1], 0.0),
                  incs=[(sV, 1)])
        wod = sW.n

        bank_guard = [0, 0, 0, 0]
        xg = {"V": 0, "T": 0, "C": 0}
        for tc in range(NTC):
            tsl = slice(TC * tc, TC * tc + TC)
            for ch in range(NCH):
                kb.dma("sync", sX, x_sb[:, ch, :], xT[b, ch, :, tsl],
                       waits=[(sV, xg["V"])])
            xw = sX.n
            for dc in range(NCH):
                bk = dc % 4
                for ic in range(NCH):
                    w = []
                    if ic == 0:
                        w = [(sC, bank_guard[bk])]
                        if dc == 0 and tc == 0:
                            w.append((sW, wod))
                    kb.op("tensor",
                          lambda e, dc=dc, ic=ic, bk=bk, tsl=tsl:
                          e.matmul(bank(bk), wo_sb[:, ic, 128 * dc:128 * dc + 128],
                                   Ofm[:, ic, tsl],
                                   start=(ic == 0), stop=(ic == NCH - 1)),
                          waits=w, incs=[(sT, 1)] if ic == NCH - 1 else [])
                kb.op("scalar",
                      lambda e, dc=dc, bk=bk:
                      e.activation(a_sb[:, dc, :], bank(bk), AF.Identity,
                                   bias=bo_sb[:, dc:dc + 1]),
                      waits=[(sT, sT.n)] + ([(sV, xg["V"])] if dc == 0 else []),
                      incs=[(sC, 1)])
                bank_guard[bk] = sC.n
            mrow = row_f[0:1, 0, :]
            rrow = row_f[0:1, 1, :]
            ln_stats(lambda ch: a_sb[:, ch, :], EPS1, 1.0, sq_scr, mrow, rrow,
                     ready=[(sC, sC.n)], ones_src=onescb)
            replicate2(mrow, rrow, mr_sb)
            mrd = sC.n
            for ch in range(NCH):
                kb.op("vector",
                      lambda e, ch=ch:
                      e.tensor_tensor(a_sb[:, ch, :], a_sb[:, ch, :],
                                      mr_sb[:, 0, :], ALU.subtract),
                      waits=[(sC, mrd), (sX, xw)], incs=[(sV, 1)])
                kb.op("vector",
                      lambda e, ch=ch:
                      e.tensor_tensor(a_sb[:, ch, :], a_sb[:, ch, :],
                                      mr_sb[:, 1, :], ALU.mult),
                      waits=[(sV, sV.n)], incs=[(sV, 1)])
                kb.op("vector",
                      lambda e, ch=ch:
                      e.tensor_tensor(x1c[:, ch, :], x_sb[:, ch, :],
                                      a_sb[:, ch, :], ALU.add),
                      waits=[(sV, sV.n)], incs=[(sV, 1)])
                kb.dma("sync", sO, x1_park[ch, :, tsl], x1c[:, ch, :],
                       waits=[(sV, sV.n)])
            mrow2 = row_f[0:1, 2, :]
            rrow2 = row_f[0:1, 3, :]
            ln_stats(lambda ch: x1c[:, ch, :], EPS1, 1.0, sq_scr, mrow2, rrow2,
                     ready=[(sV, sV.n)])
            replicate2(mrow2, rrow2, mr2_sb)
            m2d = sC.n
            for ch in range(NCH):
                kb.op("vector",
                      lambda e, ch=ch, tc=tc:
                      e.tensor_tensor(z_sb[:, ch, 1 + TC * tc:1 + TC * tc + TC],
                                      x1c[:, ch, :], mr2_sb[:, 0, :], ALU.subtract),
                      waits=[(sC, m2d)], incs=[(sV, 1)])
                kb.op("vector",
                      lambda e, ch=ch, tc=tc:
                      e.tensor_tensor(z_sb[:, ch, 1 + TC * tc:1 + TC * tc + TC],
                                      z_sb[:, ch, 1 + TC * tc:1 + TC * tc + TC],
                                      mr2_sb[:, 1, :], ALU.mult),
                      waits=[(sV, sV.n)], incs=[(sV, 1)])
            xg["V"] = sV.n
        kb.tap("x1", x1_park, [NCH, 128, L], F32)
        kb.tap("z", z_sb, [128, NCH, 1 + L], F32)
        kb.barrier("A3-done")
        kb.pop()


    def phase_F(b, l, z_sb, rotm_sb, bd8_sb, oneh_sb):
        PADL = FW + L
        kb.push()
        Qf = kb.buf([128, 4, L], BF16, tag=("qkvf", b, l))
        Kf = kb.buf([128, 4, PADL], BF16, tag=("qkvf", b, l))
        Vtm = kb.buf([128, 20, FH * 65], BF16, tag=("qkvf", b, l))
        qs_sb = kb.buf([128, 2], F32)
        lgb_sb = kb.buf([128, 2, NCH], F32)
        b1_sb = kb.buf([128, 2, NH1], F32)
        b2_sb = kb.buf([128, NCH], F32)
        kb.dma("sync", sW, qs_sb[:, 0:1], f_qs[l:l + 1, :].rearrange("o p -> p o"))
        kb.dma("sync", sW, qs_sb[:, 1:2], f_ks[l:l + 1, :].rearrange("o p -> p o"))
        for ch in range(NCH):
            kb.dma("sync", sW, lgb_sb[:, 0, ch:ch + 1],
                   f_lng[l, ch:ch + 1, :].rearrange("o p -> p o"))
            kb.dma("sync", sW, lgb_sb[:, 1, ch:ch + 1],
                   f_lnb[l, ch:ch + 1, :].rearrange("o p -> p o"))
            kb.dma("sync", sW, b2_sb[:, ch:ch + 1],
                   fw_b2[l, ch:ch + 1, :].rearrange("o p -> p o"))
        for gg in range(2):
            for kk in range(NH1):
                kb.dma("sync", sW, b1_sb[:, gg, kk:kk + 1],
                       fw_b1[l, gg, kk:kk + 1, :].rearrange("o p -> p o"))
        smalls_w = sW.n

        kb.push()
        xfb = kb.buf([128, NCH, 1 + L], BF16)
        wq_sb = kb.buf([128, 2, NCH, 128], BF16)
        vtmp = kb.buf([128, TC], BF16)
        for ch in range(NCH):
            kb.op("vector",
                  lambda e, ch=ch: e.tensor_copy(xfb[:, ch, :], z_sb[:, ch, :]),
                  incs=[(sV, 1)])
        xfbd = sV.n
        for c4 in range(4):
            kb.op("vector", lambda e, c4=c4: e.memset(Kf[:, c4, 0:FW], 0.0),
                  incs=[(sV, 1)])
        for cb in range(4):
            kb.op("vector", lambda e, cb=cb: e.memset(Vtm[:, cb, :], 0.0),
                  incs=[(sV, 1)])
        for h in range(FH):
            kb.op("vector",
                  lambda e, h=h: e.memset(Vtm[:, 4:20, 65 * h + 64:65 * h + 65], 1.0),
                  incs=[(sV, 1)])
        pads_v = sV.n

        def shifted(ch, tsl_lo):
            off = 0 if ch >= 4 else 1
            return xfb[:, ch, off + tsl_lo: off + tsl_lo + TC]

        wslot = {0: 0, 1: 0}
        for dc in range(12):
            slot = dc % 2
            pb = 4 * slot
            for ch in range(NCH):
                kb.dma("gpsimd", sWs[slot], wq_sb[:, slot, ch, :],
                       fw_qkv[l, 128 * ch:128 * ch + 128, 128 * dc:128 * dc + 128],
                       waits=[(sT, wslot[slot])])
            first_w = [(sWs[slot], sWs[slot].n), (sV, sV.n), (sC, sC.n)]
            for ch in range(NCH):
                for tc in range(NTC):
                    kb.op("tensor",
                          lambda e, ch=ch, tc=tc, slot=slot, pb=pb:
                          e.matmul(bank(pb + tc), wq_sb[:, slot, ch, :],
                                   shifted(ch, TC * tc),
                                   start=(ch == 0), stop=(ch == NCH - 1)),
                          waits=first_w if (ch == 0 and tc == 0) else [],
                          incs=[(sT, 1)] if (ch == NCH - 1 and tc == NTC - 1) else [])
            wslot[slot] = sT.n
            mmd = sT.n
            for tc in range(NTC):
                tsl = slice(TC * tc, TC * tc + TC)
                if dc < 4:
                    kb.op("scalar",
                          lambda e, dc=dc, tc=tc, pb=pb, tsl=tsl:
                          e.copy(Qf[:, dc, tsl], bank(pb + tc)),
                          waits=[(sT, mmd)], incs=[(sC, 1)])
                elif dc < 8:
                    kb.op("scalar",
                          lambda e, dc=dc, tc=tc, pb=pb, tsl=tsl:
                          e.copy(Kf[:, dc - 4, FW + TC * tc:FW + TC * tc + TC],
                                 bank(pb + tc)),
                          waits=[(sT, mmd)], incs=[(sC, 1)])
                else:
                    kb.op("scalar",
                          lambda e, tc=tc, pb=pb: e.copy(vtmp[:], bank(pb + tc)),
                          waits=[(sT, mmd), (sV, sV.n)], incs=[(sC, 1)])
                    evd = sC.n
                    ob = 4 - pb
                    for hh in range(2):
                        hD = 2 * (dc - 8) + hh
                        for blk in range(4):
                            cblk = 4 + tc * 4 + blk
                            tbk = ob + (blk % 2)
                            kb.op("tensor",
                                  lambda e, hh=hh, blk=blk, tbk=tbk:
                                  e.transpose(bankb(tbk)[0:128, 0:64],
                                              vtmp[64 * hh:64 * hh + 64,
                                                   128 * blk:128 * blk + 128],
                                              ident[64 * hh:64 * hh + 64,
                                                    64 * hh:64 * hh + 64]),
                                  waits=[(sC, evd), (sV, max(sV.n, pads_v))],
                                  incs=[(sT, 1)])
                            kb.op("vector",
                                  lambda e, hD=hD, cblk=cblk, tbk=tbk:
                                  e.tensor_copy(Vtm[:, cblk, 65 * hD:65 * hD + 64],
                                                bankb(tbk)[0:128, 0:64]),
                                  waits=[(sT, sT.n)], incs=[(sV, 1)])
        kb.pop()
        Ofml = kb.buf([128, 4, L], BF16, tag=("ofml", b, l))
        kb.barrier(f"F1-{b}-{l}")

        kb.push()
        rot_sb = kb.buf([128, 4, L], BF16)
        for j, cr in enumerate((c_rotqc, c_rotqs, c_rotkc, c_rotks)):
            kb.dma("gpsimd", sW, rot_sb[:, j, :], cr[:])
        kb.barrier(f"F2tab-{b}-{l}")
        sqf = kb.buf([128, TC], BF16)
        rn_sb = kb.buf([8, 2, TC], F32)
        rtmp = kb.buf([128, TC], BF16)
        for qk in range(2):
            tab_c = rot_sb[:, 2 * qk, :]
            tab_s = rot_sb[:, 2 * qk + 1, :]
            for tc in range(NTC):
                tsl = slice(TC * tc, TC * tc + TC)

                def src(c4, qk=qk, tsl=tsl):
                    if qk == 0:
                        return Qf[:, c4, tsl]
                    return Kf[:, c4, FW + TC * tc:FW + TC * tc + TC]

                for c4 in range(4):
                    kb.op("scalar",
                          lambda e, t=src(c4): e.activation(sqf[:], t, AF.Square),
                          waits=[(sT, sT.n)], incs=[(sC, 1)])
                    kb.op("tensor",
                          lambda e, c4=c4:
                          e.matmul(bank(6)[0:8, :], bd8_sb[:, c4, :], sqf[:],
                                   start=(c4 == 0), stop=(c4 == 3)),
                          waits=[(sC, sC.n)] + ([(sV, sV.n)] if c4 == 0 else []),
                          incs=[(sT, 1)])
                kb.op("scalar",
                      lambda e, qk=qk: e.activation(rn_sb[0:8, qk, :], bank(6)[0:8, :],
                                                    AF.Ln),
                      waits=[(sT, sT.n)], incs=[(sC, 1)])
                kb.op("scalar",
                      lambda e, qk=qk: e.activation(rn_sb[0:8, qk, :], rn_sb[0:8, qk, :],
                                                    AF.Exp, scale=-0.5),
                      waits=[(sC, sC.n)], incs=[(sC, 1)])
                kb.op("vector",
                      lambda e, qk=qk: e.tensor_scalar(rn_sb[0:8, qk, :],
                                                       rn_sb[0:8, qk, :],
                                                       1e12, None, ALU.min),
                      waits=[(sC, sC.n)], incs=[(sV, 1)])
                rnd = sV.n
                for c4 in range(4):
                    bk = c4 % 2
                    kb.op("tensor",
                          lambda e, c4=c4, bk=bk, qk=qk:
                          e.matmul(bank(bk), oneh_sb[0:8, c4, :], rn_sb[0:8, qk, :],
                                   start=True, stop=True),
                          waits=[(sV, rnd) if c4 == 0 else (sV, sV.n)],
                          incs=[(sT, 1)])
                    kb.op("vector",
                          lambda e, t=src(c4), bk=bk, qk=qk:
                          e.scalar_tensor_tensor(t, t,
                                                 qs_sb[:, qk:qk + 1], bank(bk),
                                                 ALU.mult, ALU.mult),
                          waits=[(sT, sT.n), (sW, smalls_w)], incs=[(sV, 1)])
                    kb.op("tensor",
                          lambda e, t=src(c4), bk=bk:
                          e.matmul(bank(2 + bk), rotm_sb[:], t,
                                   start=True, stop=True),
                          waits=[(sV, sV.n)], incs=[(sT, 1)])
                    kb.op("vector",
                          lambda e, c4=c4, bk=bk, tab_s=tab_s, tsl=tsl:
                          e.tensor_tensor(rtmp[:], bank(2 + bk), tab_s[:, tsl],
                                          ALU.mult),
                          waits=[(sT, sT.n)], incs=[(sV, 1)])
                    kb.op("vector",
                          lambda e, t=src(c4), tab_c=tab_c, tsl=tsl:
                          e.tensor_tensor(t, t, tab_c[:, tsl], ALU.mult),
                          waits=[(sV, sV.n)], incs=[(sV, 1)])
                    kb.op("vector",
                          lambda e, t=src(c4):
                          e.tensor_tensor(t, t, rtmp[:], ALU.add),
                          waits=[(sV, sV.n)], incs=[(sV, 1)])
        kb.pop()
        kb.barrier(f"F2-{b}-{l}")
        kb.tap(f"qrot{l}", Qf, [128, 4, L], BF16)
        kb.tap(f"krot{l}", Kf, [128, 4, PADL], BF16)

        kb.push()
        mk_sb = kb.buf([128, 8, 1024], BF16)
        for mi in range(8):
            kb.dma("gpsimd", sW, mk_sb[:, mi, :], c_masks[mi])
        kb.barrier(f"F3tab-{b}-{l}")
        NSF = 3
        Ssl = kb.buf([128, NSF, 1024], BF16)
        Zsl = kb.buf([128, NSF, 1024], BF16)
        PTl = kb.buf([128, NSF, 1024], BF16)
        negm = kb.buf([128, 4], F32)
        rrl = kb.buf([1, 2, 128], F32)
        rpl = kb.buf([64, 2, 128], BF16)
        PVl = psA[0:65, 3072:3200]
        PVlrow = psA[64:65, 3072:3200]
        RPl = psA[0:64, 3200:3328]
        g3 = {"sc_ev": [0, 0], "zt_C": [0, 0], "ss_V": [0] * NSF,
              "zs_T": [0] * NSF, "pt_T": [0] * NSF,
              "b6_V": 0, "b6_C": 0, "rp_V": 0,
              "mm_T": [0] * 999, "sub_V": [0] * 999, "exp_C": [0] * 999}
        tiles = []
        for h in range(FH):
            for w in range(NW):
                for lt in range(4):
                    tiles.append((h, w, lt))
        NTI = len(tiles)

        def T0(i):
            h, w, lt = tiles[i]
            hb = 64 * (h % 2)
            hsl = slice(hb, hb + 64)
            c4 = h // 2
            qsl = slice(FW * w + 128 * lt, FW * w + 128 * lt + 128)
            sb0 = 4 * (i % 2)
            for half in range(2):
                ww = []
                if half == 0:
                    ww = [(sV, g3["sc_ev"][i % 2])]
                kb.op("tensor",
                      lambda e, half=half, hsl=hsl, c4=c4, w=w, qsl=qsl, sb0=sb0:
                      e.matmul(psA[:, 512 * (sb0 + half):512 * (sb0 + half) + 512],
                               Qf[hsl, c4, qsl],
                               Kf[hsl, c4, FW * w + 512 * half:
                                  FW * w + 512 * half + 512],
                               start=True, stop=True),
                      waits=ww, incs=[(sT, 1)] if half == 1 else [])
            g3["mm_T"][i] = sT.n

        def T1(i):
            h, w, lt = tiles[i]
            sb0 = 4 * (i % 2)
            sf = i % NSF
            nm = i % 4
            mi = (4 if w == 0 else 0) + lt
            kb.op("vector",
                  lambda e, sf=sf, mi=mi, sb0=sb0:
                  e.tensor_tensor(Ssl[:, sf, :], psA[:, 512 * sb0:512 * sb0 + 1024],
                                  mk_sb[:, mi, :], ALU.add),
                  waits=[(sT, g3["mm_T"][i]), (sV, g3["ss_V"][sf])],
                  incs=[(sV, 1)])
            g3["sc_ev"][i % 2] = sV.n
            kb.op("vector",
                  lambda e, sf=sf, nm=nm:
                  e.tensor_reduce(negm[:, nm:nm + 1], Ssl[:, sf, :],
                                  AX.X, ALU.max, negate=True),
                  waits=[(sV, sV.n)], incs=[(sV, 1)])
            kb.op("vector",
                  lambda e, sf=sf, nm=nm:
                  e.tensor_scalar(Zsl[:, sf, :], Ssl[:, sf, :],
                                  negm[:, nm:nm + 1], None, ALU.add),
                  waits=[(sV, sV.n), (sT, g3["zs_T"][sf])],
                  incs=[(sV, 1)])
            g3["ss_V"][sf] = sV.n
            g3["sub_V"][i] = sV.n

        def T2(i):
            sf = i % NSF
            ztb = 2 + (i % 2)
            for k in range(8):
                ww = []
                if k == 0:
                    ww = [(sV, g3["sub_V"][i]), (sC, g3["zt_C"][i % 2])]
                kb.op("tensor",
                      lambda e, k=k, sf=sf, ztb=ztb:
                      e.transpose(bankb(ztb)[:, 128 * k:128 * k + 128],
                                  Zsl[:, sf, 128 * k:128 * k + 128],
                                  ident[:]),
                      waits=ww, incs=[(sT, 1)] if k == 7 else [])
            g3["zs_T"][sf] = sT.n
            kb.op("scalar",
                  lambda e, sf=sf, ztb=ztb:
                  e.activation(PTl[:, sf, :], bankb(ztb), AF.Exp),
                  waits=[(sT, sT.n), (sT, g3["pt_T"][sf])], incs=[(sC, 1)])
            g3["zt_C"][i % 2] = sC.n
            g3["exp_C"][i] = sC.n

        def T3(i):
            h, w, lt = tiles[i]
            hb = 64 * (h % 2)
            hsl = slice(hb, hb + 64)
            c4 = h // 2
            qsl = slice(FW * w + 128 * lt, FW * w + 128 * lt + 128)
            sf = i % NSF
            pi = i % 2
            for mc in range(8):
                ww = []
                if mc == 0:
                    ww = [(sC, g3["exp_C"][i]), (sV, g3["b6_V"]), (sC, g3["b6_C"])]
                kb.op("tensor",
                      lambda e, mc=mc, h=h, w=w, sf=sf:
                      e.matmul(PVl, Vtm[:, 4 * w + mc, 65 * h:65 * h + 65],
                               PTl[:, sf, 128 * mc:128 * mc + 128],
                               start=(mc == 0), stop=(mc == 7)),
                      waits=ww, incs=[(sT, 1)] if mc == 7 else [])
            g3["pt_T"][sf] = sT.n
            kb.op("vector",
                  lambda e, pi=pi: e.reciprocal(rrl[0:1, pi, :], PVlrow),
                  waits=[(sT, sT.n)], incs=[(sV, 1)])
            kb.op("tensor",
                  lambda e, pi=pi:
                  e.matmul(RPl, ones1f[0:1, 0:64], rrl[0:1, pi, :],
                           start=True, stop=True),
                  waits=[(sV, sV.n), (sC, g3["b6_C"])], incs=[(sT, 1)])
            kb.op("scalar",
                  lambda e, pi=pi: e.copy(rpl[:, pi, :], RPl),
                  waits=[(sT, sT.n)], incs=[(sC, 1)])
            g3["b6_C"] = sC.n
            kb.op("vector",
                  lambda e, hsl=hsl, c4=c4, qsl=qsl, pi=pi:
                  e.tensor_tensor(Ofml[hsl, c4, qsl],
                                  psA[0:64, 3072:3200],
                                  rpl[:, pi, :], ALU.mult),
                  waits=[(sC, sC.n)], incs=[(sV, 1)])
            g3["b6_V"] = sV.n

        for cyc in range(NTI + 3):
            if cyc < NTI:
                T0(cyc)
            if 1 <= cyc < NTI + 1:
                T1(cyc - 1)
            if 2 <= cyc < NTI + 2:
                T2(cyc - 2)
            if 3 <= cyc < NTI + 3:
                T3(cyc - 3)
        kb.pop()
        kb.barrier(f"F3-{b}-{l}")
        kb.tap(f"ol{l}", Ofml, [128, 4, L], BF16)
        kb.free_tag(("qkvf", b, l))

        kb.push()
        fwo_sb = kb.buf([128, 4, D], BF16)
        for ic in range(4):
            kb.dma("gpsimd", sW, fwo_sb[:, ic, :],
                   fw_wo[l, 128 * ic:128 * ic + 128, :])
        fwod = sW.n
        bank_guard4 = [0, 0, 0, 0]
        for tc in range(NTC):
            tsl = slice(TC * tc, TC * tc + TC)
            for dc in range(NCH):
                bk = dc % 4
                for ic in range(4):
                    ww = []
                    if ic == 0:
                        ww = [(sV, bank_guard4[bk])]
                        if dc == 0 and tc == 0:
                            ww.append((sW, fwod))
                    kb.op("tensor",
                          lambda e, dc=dc, ic=ic, bk=bk, tsl=tsl:
                          e.matmul(bank(bk), fwo_sb[:, ic, 128 * dc:128 * dc + 128],
                                   Ofml[:, ic, tsl],
                                   start=(ic == 0), stop=(ic == 3)),
                          waits=ww, incs=[(sT, 1)] if ic == 3 else [])
                kb.op("vector",
                      lambda e, dc=dc, bk=bk, tc=tc:
                      e.tensor_tensor(z_sb[:, dc, 1 + TC * tc:1 + TC * tc + TC],
                                      psA[:, 512 * bk:512 * bk + 512],
                                      z_sb[:, dc, 1 + TC * tc:1 + TC * tc + TC],
                                      ALU.add),
                      waits=[(sT, sT.n)], incs=[(sV, 1)])
                bank_guard4[bk] = sV.n
        kb.pop()
        kb.barrier(f"F4-{b}-{l}")
        kb.tap(f"mid{l}", z_sb, [128, NCH, 1 + L], F32)
        kb.free_tag(("ofml", b, l))

        kb.push()
        h_sb = kb.buf([128, NH1, L // 2], BF16)
        hn_sb = kb.buf([128, NCH, TC], BF16)
        sq5 = kb.buf([128, TC], F32)
        mr5 = kb.buf([128, 2, TC], BF16)
        glt = kb.buf([128, 2, TC], BF16)
        w1_sb = kb.buf([128, 2, NCH * 256], BF16)
        w2_sb = kb.buf([128, 2, NH1 * 128], BF16)

        def shifted_z(ch, lo):
            off = 0 if ch >= 4 else 1
            return z_sb[:, ch, off + lo: off + lo + TC]

        w1slot = {0: 0, 1: 0}
        w2slot = {0: 0, 1: 0}
        for half in range(2):
            for tc in (2 * half, 2 * half + 1):
                tci = tc - 2 * half
                tlo = TC * tc
                mrow = row_f[0:1, 0, :]
                rrow = row_f[0:1, 1, :]
                ln_stats(lambda ch: shifted_z(ch, tlo), EPS1, 1.0, sq5, mrow, rrow,
                         ready=[(sV, sV.n)])
                replicate2(mrow, rrow, mr5)
                mrd = sC.n
                for ch in range(NCH):
                    kb.op("vector",
                          lambda e, ch=ch, tlo=tlo:
                          e.tensor_tensor(hn_sb[:, ch, :], shifted_z(ch, tlo),
                                          mr5[:, 0, :], ALU.subtract),
                          waits=[(sC, mrd)], incs=[(sV, 1)])
                    kb.op("vector",
                          lambda e, ch=ch:
                          e.tensor_tensor(hn_sb[:, ch, :], hn_sb[:, ch, :],
                                          mr5[:, 1, :], ALU.mult),
                          waits=[(sV, sV.n)], incs=[(sV, 1)])
                    kb.op("vector",
                          lambda e, ch=ch:
                          e.tensor_scalar(hn_sb[:, ch, :], hn_sb[:, ch, :],
                                          lgb_sb[:, 0, ch:ch + 1], lgb_sb[:, 1, ch:ch + 1],
                                          ALU.mult, ALU.add),
                          waits=[(sV, sV.n), (sW, smalls_w)], incs=[(sV, 1)])
                hnd = sV.n
                for kk in range(NH1):
                    slot = kk % 2
                    for gg in range(2):
                        kb.dma("gpsimd", sWs[slot],
                               w1_sb[:, slot, :].rearrange(
                                   "p (c g m) -> p c g m",
                                   c=NCH, g=2, m=128)[:, :, gg, :],
                               fw_w1[l, :, gg, 128 * kk:128 * kk + 128].rearrange(
                                   "(c p) m -> p c m", p=128),
                               waits=[(sT, w1slot[slot])] if gg == 0 else [])
                    w1ld = (sWs[slot], sWs[slot].n)
                    for gg in range(2):
                        bk = 2 * slot + gg
                        for ch in range(NCH):
                            ww = []
                            if ch == 0:
                                ww = [w1ld, (sV, hnd)]
                                gsem, gval = kb._f5g.get(bk, (sC, 0))
                                ww.append((gsem, gval))
                            kb.op("tensor",
                                  lambda e, gg=gg, ch=ch, bk=bk, slot=slot:
                                  e.matmul(bank(bk),
                                           w1_sb[:, slot, (ch * 2 + gg) * 128:
                                                 (ch * 2 + gg) * 128 + 128],
                                           hn_sb[:, ch, :],
                                           start=(ch == 0), stop=(ch == NCH - 1)),
                                  waits=ww, incs=[(sT, 1)] if ch == NCH - 1 else [])
                    w1slot[slot] = sT.n
                    w1mm = sT.n
                    kb.op("scalar",
                          lambda e, kk=kk, slot=slot:
                          e.activation(glt[:, slot, :], bank(2 * slot + 1), AF.Gelu,
                                       bias=b1_sb[:, 1, kk:kk + 1]),
                          waits=[(sT, w1mm)], incs=[(sC, 1)])
                    kb._f5g[2 * slot + 1] = (sC, sC.n)
                    kb.op("vector",
                          lambda e, kk=kk, slot=slot, tci=tci:
                          e.scalar_tensor_tensor(h_sb[:, kk, TC * tci:TC * tci + TC],
                                                 bank(2 * slot),
                                                 b1_sb[:, 0, kk:kk + 1], glt[:, slot, :],
                                                 ALU.add, ALU.mult),
                          waits=[(sC, sC.n)], incs=[(sV, 1)])
                    kb._f5g[2 * slot] = (sV, sV.n)
            if half == 0:
                kb.tap(f"glu{l}", h_sb, [128, NH1, L // 2], BF16)
            hall = sV.n
            w2bg = [0, 0]
            for dc in range(NCH):
                slot = dc % 2
                kb.dma("gpsimd", sWs[slot],
                       w2_sb[:, slot, :].rearrange("p (k m) -> p k m",
                                                   k=NH1, m=128),
                       fw_w2[l, :, 128 * dc:128 * dc + 128].rearrange(
                           "(k p) m -> p k m", p=128),
                       waits=[(sT, w2slot[slot])])
                w2ld = (sWs[slot], sWs[slot].n)
                for tci in range(2):
                    bk = 4 + 2 * slot + tci
                    for kk in range(NH1):
                        ww = []
                        if kk == 0:
                            ww = [w2ld, (sV, max(hall, w2bg[tci])), (sC, sC.n)]
                        kb.op("tensor",
                              lambda e, slot=slot, kk=kk, bk=bk, tci=tci:
                              e.matmul(bank(bk),
                                       w2_sb[:, slot, 128 * kk:128 * kk + 128],
                                       h_sb[:, kk, TC * tci:TC * tci + TC],
                                       start=(kk == 0), stop=(kk == NH1 - 1)),
                              waits=ww, incs=[(sT, 1)] if kk == NH1 - 1 else [])
                w2slot[slot] = sT.n
                for tci in range(2):
                    bk = 4 + 2 * slot + tci
                    tc = 2 * half + tci
                    kb.op("vector",
                          lambda e, dc=dc, bk=bk, tc=tc:
                          e.scalar_tensor_tensor(z_sb[:, dc, 1 + TC * tc:1 + TC * tc + TC],
                                                 bank(bk), b2_sb[:, dc:dc + 1],
                                                 z_sb[:, dc, 1 + TC * tc:1 + TC * tc + TC],
                                                 ALU.add, ALU.add),
                          waits=[(sT, w2slot[slot])], incs=[(sV, 1)])
                    w2bg[tci] = sV.n
        kb.pop()
        kb.barrier(f"F5-{b}-{l}")
        kb.tap(f"fout{l}", z_sb, [128, NCH, 1 + L], F32)
        kb.pop()


    def phase_OUT(b, z_sb):
        kb.push()
        x1_sb = kb.buf([128, NCH, TC], F32)
        oc = kb.buf([128, NCH, TC], F32)
        sqo = kb.buf([128, TC], F32)
        mro = kb.buf([128, 2, TC], BF16)
        xg = 0
        for tc in range(NTC):
            tsl = slice(TC * tc, TC * tc + TC)
            for ch in range(NCH):
                kb.dma("sync", sX, x1_sb[:, ch, :], x1_park[ch, :, tsl],
                       waits=[(sV, xg)])
            x1w = sX.n
            mrow = row_f[0:1, 0, :]
            rrow = row_f[0:1, 1, :]
            ln_stats(lambda ch: z_sb[:, ch, 1 + TC * tc:1 + TC * tc + TC],
                     EPS1, 1.0, sqo, mrow, rrow, ready=[(sV, sV.n)])
            replicate2(mrow, rrow, mro)
            mrd = sC.n
            for ch in range(NCH):
                kb.op("vector",
                      lambda e, ch=ch, tc=tc:
                      e.tensor_tensor(oc[:, ch, :],
                                      z_sb[:, ch, 1 + TC * tc:1 + TC * tc + TC],
                                      mro[:, 0, :], ALU.subtract),
                      waits=[(sC, mrd), (sX, x1w)], incs=[(sV, 1)])
                kb.op("vector",
                      lambda e, ch=ch:
                      e.tensor_tensor(oc[:, ch, :], oc[:, ch, :],
                                      mro[:, 1, :], ALU.mult),
                      waits=[(sV, sV.n)], incs=[(sV, 1)])
                kb.op("vector",
                      lambda e, ch=ch:
                      e.tensor_tensor(oc[:, ch, :], x1_sb[:, ch, :],
                                      oc[:, ch, :], ALU.add),
                      waits=[(sV, sV.n)], incs=[(sV, 1)])
                kb.dma("sync", sO, outT[b, ch, :, tsl], oc[:, ch, :],
                       waits=[(sV, sV.n)])
            xg = sV.n
        kb.barrier("OUT-done")
        kb.pop()

    kb._f5g = {}
    batches = [0] if (DEV_PHASES is not None and DEV_PHASES != {"ALL"}) else [0, 1]
    if DEV_BATCHES is not None:
        batches = DEV_BATCHES
    full = DEV_PHASES is None or DEV_PHASES == {"ALL"}
    for b in batches:
        kb.push()
        if full or ph_on("A1") or ph_on("A2") or ph_on("A3"):
            Q1_sb = kb.buf([128, NCH, L], BF16, tag=("qkv", b))
            K_sb = kb.buf([128, NCH, L], BF16, tag=("qkv", b))
            V_tm = kb.buf([128, NLT, H * 65], BF16, tag=("qkv", b))
            Ofm = kb.buf([128, NCH, L], BF16, tag=("ofm", b))
            if full or ph_on("A1"):
                phase_A1(b, Q1_sb, K_sb, V_tm)
            if full or ph_on("A2"):
                phase_A2(b, Q1_sb, K_sb, V_tm, Ofm)
            kb.free_tag(("qkv", b))
            if DEV_OUT == "ofm":
                for ch in range(NCH):
                    kb.dma("gpsimd", sO, outT[b, ch, :, :], Ofm[:, ch, :])
                kb.barrier("ofmdump")
                kb.free_tag(("ofm", b))
                kb.pop()
                continue
            if full or ph_on("A3"):
                z_sb = kb.buf([128, NCH, 1 + L], F32, tag=("z", b))
                phase_A3(b, Ofm, z_sb)
            kb.free_tag(("ofm", b))
            if DEV_OUT == "z":
                for ch in range(NCH):
                    kb.dma("sync", sO, outT[b, ch, :, :], z_sb[:, ch, 1:])
                kb.barrier("zdump")
                kb.free_tag(("z", b))
                kb.pop()
                continue
        if full or ph_on("F"):
            if not (full or ph_on("A3")):
                z_sb = kb.buf([128, NCH, 1 + L], F32, tag=("z", b))
            rotm_sb = kb.buf([128, 128], BF16, tag=("ftab", b))
            bd8_sb = kb.buf([128, 4, 8], BF16, tag=("ftab", b))
            oneh_sb = kb.buf([8, 4, 128], F32, tag=("ftab", b))
            kb.dma("gpsimd", sW, rotm_sb, c_rotm[:])
            for c4 in range(4):
                kb.dma("gpsimd", sW, bd8_sb[:, c4, :], c_bd8[c4])
                kb.dma("sync", sW, oneh_sb[0:8, c4, :], c_oneh[c4])
            for l in range(2):
                kb.push()
                phase_F(b, l, z_sb, rotm_sb, bd8_sb, oneh_sb)
                kb.pop()
            kb.free_tag(("ftab", b))
            if full or ph_on("OUT"):
                phase_OUT(b, z_sb)
        kb.free_tag(("z", b))
        kb.pop()

    kb.barrier("end")
    kb.op("sync", None, [(sO, sO.n), (kb.tap_sem, kb.tap_sem.n)], [])
    return kb, {"outT": outT}


def _host_prep(inputs):
    c = host_constants()
    x = np.ascontiguousarray(inputs["L0_attn_x"])
    base = {
        "Wq": inputs["L0_attn_Wq"], "Wk": inputs["L0_attn_Wk"],
        "Wv": inputs["L0_attn_Wv"], "Wr": inputs["L0_attn_Wr"],
        "Wo": inputs["L0_attn_Wo"],
        "gamma8": np.asarray(inputs["L0_attn_gamma"]).reshape(NCH, 128),
        "beta8": np.asarray(inputs["L0_attn_beta"]).reshape(NCH, 128),
        "bo8": np.asarray(inputs["L0_attn_bo"]).reshape(NCH, 128),
        "ub8": np.asarray(inputs["L0_attn_u_bias"]).reshape(NCH, 128),
        "vb8": np.asarray(inputs["L0_attn_v_bias"]).reshape(NCH, 128),
        "rel": np.ascontiguousarray(np.asarray(inputs["L0_attn_rel_pos_emb"])[2953:7048]),
        "fw_qkv": inputs["L0_ffn_wqkv"],
        "fw_wo": inputs["L0_ffn_wo"],
        "f_qs": np.tile(np.asarray(inputs["L0_ffn_q_scale"]), (1, 2)),
        "f_ks": np.tile(np.asarray(inputs["L0_ffn_k_scale"]), (1, 2)),
        "f_lng": np.asarray(inputs["L0_ffn_ln_g"]).reshape(2, NCH, 128),
        "f_lnb": np.asarray(inputs["L0_ffn_ln_b"]).reshape(2, NCH, 128),
        "fw_b2": np.asarray(inputs["L0_ffn_b2"]).reshape(2, NCH, 128),
        "c_ident": c["ident"], "c_ones1": c["ones1"],
        "c_rotqc": c["rotqc"], "c_rotqs": c["rotqs"],
        "c_rotkc": c["rotkc"], "c_rotks": c["rotks"],
        "c_rotm": c["rotm"], "c_masks": c["masks"], "c_fpool": c["fpool"],
        "c_bd8": c["bd8"], "c_oneh": c["oneh"],
    }
    w1 = np.asarray(inputs["L0_ffn_w1"])
    b1 = np.asarray(inputs["L0_ffn_b1"])
    w2 = np.asarray(inputs["L0_ffn_w2"])
    w1p = np.zeros((2, D, 2, HID_PAD), np.float32)
    w1p[:, :, 0, :2730] = w1[:, :, :2730]
    w1p[:, :, 1, :2730] = w1[:, :, 2730:]
    b1p = np.zeros((2, 2, HID_PAD), np.float32)
    b1p[:, 0, :2730] = b1[:, :2730]
    b1p[:, 1, :2730] = b1[:, 2730:]
    w2p = np.zeros((2, HID_PAD, D), np.float32)
    w2p[:, :2730, :] = w2
    base["fw_w1"] = w1p
    base["fw_b1"] = b1p.reshape(2, 2, NH1, 128)
    base["fw_w2"] = w2p
    base = {k: np.ascontiguousarray(v, dtype=np.float32) for k, v in base.items()}

    cores = DEV_CORES if DEV_CORES else list(range(N_CORES))
    per_core = []
    for core in cores:
        m = dict(base)
        xc = x[2 * core: 2 * core + 2]
        m["xT"] = np.ascontiguousarray(xc.transpose(0, 2, 1).reshape(2, NCH, 128, L))
        per_core.append(m)
    return per_core, cores


def kernel(**inputs):
    per_core, cores = _host_prep(inputs)
    kb, info = build_program()
    kb.build()
    res = run_bass_kernel_spmd(kb.nc, per_core, core_ids=cores)
    kernel.last_results = res.results
    outs = []
    for r in res.results:
        o = np.asarray(r["outT"]).reshape(2, D, L).transpose(0, 2, 1)
        outs.append(o)
    return np.concatenate(outs, axis=0).astype(np.float32)


# version 14 (final)
# speedup vs baseline: 1.0048x; 1.0048x over previous
import contextlib
import numpy as np
import ml_dtypes
import concourse.bass as bass
import concourse.mybir as mybir
from concourse.bass_utils import run_bass_kernel_spmd

F32 = mybir.dt.float32
BF16 = mybir.dt.bfloat16
AF = mybir.ActivationFunctionType
ALU = mybir.AluOpType
AX = mybir.AxisListType
bfd = ml_dtypes.bfloat16

TAPS = []
DEV_OUT = None
DEV_A2N = None
DEV_BATCHES = None
DEV_PHASES = None
DEV_CORES = None
N_CORES = 8

D = 1024
L = 2048
H = 16
DH = 64
NCH = 8
TC = 512
NTC = L // TC
NLT = L // 128
JW = 2176
EPS1 = 1e-5
EPS2 = 1e-6
FH = 8
FW = 512
NW = L // FW
FIN = 512
NH1 = 22
HID_PAD = 2816
ENGS = ("sync", "tensor", "vector", "scalar", "gpsimd")
ARENA_BYTES = 190 * 1024
FCONSTS = [EPS1 * EPS2, 1.0 + EPS2, EPS1, -0.5, 0.125, 1.0, 1.0 / D, 1e12, 0.5]


class Sem:
    __slots__ = ("h", "n", "name")

    def __init__(self, h, name):
        self.h = h
        self.n = 0
        self.name = name

    def bump(self, k):
        self.n += k
        return self.n


class KB:
    def __init__(self):
        self.nc = bass.Bass("TRN2", target_bir_lowering=False)
        self.sched = {e: [] for e in ENGS}
        self.es = contextlib.ExitStack()
        self.sems = {}
        self.tap_tensors = {}
        self.tap_sem = None
        self.n_ops = 0
        self.arena = None
        self.ar_off = 0
        self.ar_stack = []
        self.ar_views = {}
        self.dma_sems = set()
        self.free_list = [(0, ARENA_BYTES)]
        self.tags = {}
        self.live = []

    def sem(self, name):
        s = Sem(self.es.enter_context(self.nc.semaphore(name)), name)
        self.sems[name] = s
        return s

    def init_arena(self):
        t = self.es.enter_context(
            self.nc.sbuf_tensor("ARENA", [128, ARENA_BYTES], mybir.dt.uint8))
        self.ar_views = {F32: t.ap().bitcast(F32), BF16: t.ap().bitcast(BF16)}

    def push(self):
        self.ar_stack.append([])

    def pop(self):
        for off, nb in self.ar_stack.pop():
            self._free_range(off, nb)

    def _alloc_range(self, nbytes):
        nbytes = -(-nbytes // 32) * 32
        for i, (off, sz) in enumerate(self.free_list):
            if sz >= nbytes:
                if sz == nbytes:
                    self.free_list.pop(i)
                else:
                    self.free_list[i] = (off + nbytes, sz - nbytes)
                return off, nbytes
        raise AssertionError(
            f"SBUF arena overflow: need {nbytes}, free={self.free_list}")

    def _free_range(self, off, nbytes):
        self.live = [(o, n) for (o, n) in self.live if not (o == off and n == nbytes)]
        self.free_list.append((off, nbytes))
        self.free_list.sort()
        merged = []
        for o, s in self.free_list:
            if merged and merged[-1][0] + merged[-1][1] == o:
                merged[-1] = (merged[-1][0], merged[-1][1] + s)
            else:
                merged.append((o, s))
        self.free_list = [tuple(x) for x in merged]

    def buf(self, shape, dtype, tag=None):
        v, _ = self.buf2(shape, dtype, tag)
        return v

    def buf2(self, shape, dtype, tag=None):
        esz = 4 if dtype is F32 else 2
        free = 1
        for d in shape[1:]:
            free *= d
        off, nbytes = self._alloc_range(free * esz)
        for o2, n2 in self.live:
            assert off + nbytes <= o2 or o2 + n2 <= off, \
                f"ALLOC OVERLAP: [{off},{off+nbytes}) vs [{o2},{o2+n2})"
        self.live.append((off, nbytes))
        if tag is not None:
            self.tags.setdefault(tag, []).append((off, nbytes))
        elif self.ar_stack:
            self.ar_stack[-1].append((off, nbytes))
        v = self.ar_views[dtype][0:shape[0], off // esz: off // esz + free]
        if len(shape) == 3:
            v = v.rearrange("p (a b) -> p a b", a=shape[1], b=shape[2])
        elif len(shape) == 4:
            v = v.rearrange("p (a b c) -> p a b c", a=shape[1], b=shape[2], c=shape[3])
        return v, off // esz

    def free_tag(self, tag):
        for off, nb in self.tags.pop(tag, []):
            self._free_range(off, nb)

    def diag_ap(self, dtype, elem_off, npart, width):
        t = self.ar_views[dtype].tensor
        fs = ARENA_BYTES // (4 if dtype is F32 else 2)
        return bass.AP(t, elem_off + npart - 1, [[fs - 1, npart], [1, width]])

    def gsb(self, name, shape, dtype):
        return self.es.enter_context(self.nc.sbuf_tensor(name, shape, dtype)).ap()

    def op(self, eng, fn, waits=(), incs=(), is_dma=False):
        for s, v in waits:
            assert v <= s.n, f"wait on future inc of {s.name}: {v} > {s.n}"
        for s, k in incs:
            s.bump(k)
        ent = [tuple(waits), fn, list(incs), is_dma]
        self.sched[eng].append(ent)
        self.n_ops += 1
        return ent

    def dma(self, eng, sem, out, in_, waits=(), **kw):
        self.dma_sems.add(sem)
        self.op(eng, lambda e, o=out, i=in_, kw=kw: e.dma_start(out=o, in_=i, **kw),
                waits, [(sem, 16)], is_dma=True)
        return sem.n

    def barrier(self, tag=""):
        for e in ("tensor", "vector", "scalar", "gpsimd"):
            s = self.sems["e_" + e]
            for ent in reversed(self.sched[e]):
                if ent[3]:
                    continue
                if ent[1] is None:
                    break
                if not any(ss is s for ss, _ in ent[2]):
                    s.bump(1)
                    ent[2].append((s, 1))
                break
        snap = [(self.sems["e_" + e], self.sems["e_" + e].n) for e in ENGS]
        snap += [(s, s.n) for s in self.dma_sems]
        for e in ENGS:
            waits = [(s, v) for s, v in snap
                     if s is not self.sems["e_" + e] and v > 0]
            self.op(e, None, waits, [])

    def tap(self, name, ap, shape, dtype=F32):
        if name not in TAPS or name in self.tap_tensors:
            return
        t = self.nc.dram_tensor("tap_" + name, list(shape), dtype,
                                kind="ExternalOutput").ap()
        self.tap_tensors[name] = t
        waits = [(self.sems["e_" + e], self.sems["e_" + e].n) for e in ENGS
                 if self.sems["e_" + e].n > 0]
        self.dma("sync", self.tap_sem, t, ap, waits=waits)

    def build(self):
        sched = self.sched

        def run(e, name):
            for waits, fn, incs, _dma in sched[name]:
                for s, v in waits:
                    e.wait_ge(s.h, v)
                ins = fn(e) if fn is not None else None
                if incs:
                    if ins is None:
                        ins = e.nop()
                    for s, k in incs:
                        ins.then_inc(s.h, k)

        with self.nc.Block() as block:
            block.sync(lambda e: run(e, "sync"))
            block.tensor(lambda e: run(e, "tensor"))
            block.vector(lambda e: run(e, "vector"))
            block.scalar(lambda e: run(e, "scalar"))
            block.gpsimd(lambda e: run(e, "gpsimd"))



def host_constants():
    c = {}
    c["ident"] = np.eye(128, dtype=np.float32)
    c["ones1"] = np.ones((1, 128), np.float32)
    c["fpool"] = np.tile(np.asarray(FCONSTS, np.float32)[None, :], (128, 1))
    half = np.arange(0, DH, 2, dtype=np.float64)
    inv_freq = 1.0 / 10000.0 ** (half / DH)
    t = np.arange(L, dtype=np.float64)
    fr32 = t[:, None] * inv_freq[None, :]
    base = (half + 0.4 * DH) / (1.4 * DH)
    power = (t - L // 2) / float(FW // 2)
    sc32 = base[None, :] ** power[:, None]
    cos64 = np.cos(np.concatenate([fr32, fr32], 1))
    sin64 = np.sin(np.concatenate([fr32, fr32], 1))
    sc64 = np.concatenate([sc32, sc32], 1)
    QKS = 8.0
    c["rotqc"] = np.tile((cos64 * sc64).T * QKS, (2, 1)).astype(np.float32)
    c["rotqs"] = np.tile((sin64 * sc64).T * QKS, (2, 1)).astype(np.float32)
    c["rotkc"] = np.tile((cos64 / sc64).T, (2, 1)).astype(np.float32)
    c["rotks"] = np.tile((sin64 / sc64).T, (2, 1)).astype(np.float32)
    rotm = np.zeros((128, 128), np.float32)
    for blk in range(2):
        o = 64 * blk
        for dh in range(32):
            rotm[o + dh + 32, o + dh] = -1.0
            rotm[o + dh, o + dh + 32] = 1.0
    c["rotm"] = rotm
    masks = np.zeros((8, 128, 1024), np.float32)
    for w0 in range(2):
        for lt in range(4):
            l = lt * 128 + np.arange(128)[:, None]
            jj = np.arange(1024)[None, :] - 512
            bad = (l < jj) | (l > jj + FW)
            if w0:
                bad = bad | (jj < 0)
            masks[w0 * 4 + lt][bad] = -1e30
    c["masks"] = masks
    bd8 = np.zeros((4, 128, 8), np.float32)
    oneh = np.zeros((4, 8, 128), np.float32)
    for cc in range(4):
        for p in range(128):
            j = 2 * cc + p // 64
            bd8[cc, p, j] = 1.0
            oneh[cc, j, p] = 1.0
    c["bd8"] = bd8
    c["oneh"] = oneh
    return c



def build_program():
    kb = KB()
    nc = kb.nc

    def dinp(name, shape, dtype=F32):
        return nc.dram_tensor(name, list(shape), dtype, kind="ExternalInput").ap()

    xT = dinp("xT", [2, NCH, 128, L])
    Wq = dinp("Wq", [D, D]); Wk = dinp("Wk", [D, D]); Wv = dinp("Wv", [D, D])
    Wr = dinp("Wr", [D, D]); Wo = dinp("Wo", [D, D])
    gamma8 = dinp("gamma8", [NCH, 128]); beta8 = dinp("beta8", [NCH, 128])
    bo8 = dinp("bo8", [NCH, 128])
    ub8 = dinp("ub8", [NCH, 128]); vb8 = dinp("vb8", [NCH, 128])
    rel = dinp("rel", [4095, D])
    fw_qkv = dinp("fw_qkv", [2, D, 3 * FIN])
    fw_wo = dinp("fw_wo", [2, FIN, D])
    fw_w1 = dinp("fw_w1", [2, D, 2, HID_PAD])
    fw_b1 = dinp("fw_b1", [2, 2, NH1, 128])
    fw_w2 = dinp("fw_w2", [2, HID_PAD, D])
    fw_b2 = dinp("fw_b2", [2, NCH, 128])
    f_qs = dinp("f_qs", [2, 128]); f_ks = dinp("f_ks", [2, 128])
    f_lng = dinp("f_lng", [2, NCH, 128]); f_lnb = dinp("f_lnb", [2, NCH, 128])
    c_ident = dinp("c_ident", [128, 128])
    c_ones1 = dinp("c_ones1", [1, 128])
    c_rotqc = dinp("c_rotqc", [128, L]); c_rotqs = dinp("c_rotqs", [128, L])
    c_rotkc = dinp("c_rotkc", [128, L]); c_rotks = dinp("c_rotks", [128, L])
    c_rotm = dinp("c_rotm", [128, 128])
    c_masks = dinp("c_masks", [8, 128, 1024])
    c_fpool = dinp("c_fpool", [128, len(FCONSTS)])
    c_bd8 = dinp("c_bd8", [4, 128, 8])
    c_oneh = dinp("c_oneh", [4, 8, 128])

    outT = nc.dram_tensor("outT", [2, NCH, 128, L], F32, kind="ExternalOutput").ap()
    rt_scr = nc.dram_tensor("rt_scr", [D, 4352], BF16, kind="Internal").ap()
    x1_park = nc.dram_tensor("x1_park", [NCH, 128, L], F32, kind="Internal").ap()

    for e in ENGS:
        kb.sem("e_" + e)
    sS = kb.sems["e_sync"]
    sT = kb.sems["e_tensor"]
    sV = kb.sems["e_vector"]
    sC = kb.sems["e_scalar"]
    sG = kb.sems["e_gpsimd"]
    sW = kb.sem("wload")
    sX = kb.sem("xload")
    sD = kb.sem("gdiag")
    sO = kb.sem("outs")
    sWs = [kb.sem("wslot0"), kb.sem("wslot1")]
    sRTs = [kb.sem("rtslot0"), kb.sem("rtslot1")]
    sDs = [kb.sem("dslot0"), kb.sem("dslot1"), kb.sem("dslot2")]
    kb.tap_sem = kb.sem("taps")

    kb.init_arena()
    ident = kb.gsb("ident", [128, 128], BF16)
    ones1b = kb.gsb("ones1b", [1, 128], BF16)
    ones1f = kb.gsb("ones1f", [1, 128], F32)
    onescf = kb.gsb("onescf", [128, 1], F32)
    onescb = kb.gsb("onescb", [128, 1], BF16)
    fpool = kb.gsb("fpool", [128, len(FCONSTS)], F32)
    gam_sb = kb.gsb("gam_sb", [128, NCH], F32)
    bet_sb = kb.gsb("bet_sb", [128, NCH], F32)
    bo_sb = kb.gsb("bo_sb", [128, NCH], F32)
    ub_sb = kb.gsb("ub_sb", [128, NCH], F32)
    vb_sb = kb.gsb("vb_sb", [128, NCH], F32)
    dvb_sb = kb.gsb("dvb_sb", [128, NCH], F32)
    row_f = kb.gsb("row_f", [1, 4, TC], F32)

    psA_t = kb.es.enter_context(nc.psum_tensor("psA", [128, 4096], F32))
    psA = psA_t.ap()
    psAb = psA.bitcast(BF16)

    def bank(i, nf32=512):
        return psA[:, 512 * i: 512 * i + nf32]

    def bankb(i, nbf=1024):
        return psAb[:, 1024 * i: 1024 * i + nbf]

    kb.dma("gpsimd", sW, ident[:], c_ident[:])
    kb.dma("gpsimd", sW, ones1b[0:1, :], c_ones1[:])
    kb.dma("sync", sW, ones1f[0:1, :], c_ones1[:])
    kb.dma("sync", sW, onescf[:, 0:1], c_ones1.rearrange("o p -> p o"))
    kb.dma("gpsimd", sW, onescb[:, 0:1], c_ones1.rearrange("o p -> p o"))
    kb.dma("sync", sW, fpool[:], c_fpool[:])
    for j, v in enumerate(FCONSTS):
        nc.const_aps.register(F32, float(np.float32(v)), fpool[:, j:j + 1])
        nc.const_aps.register(F32, float(v), fpool[:, j:j + 1])

    for i in range(NCH):
        kb.dma("sync", sW, gam_sb[:, i:i + 1], gamma8[i:i + 1, :].rearrange("o p -> p o"))
        kb.dma("sync", sW, bet_sb[:, i:i + 1], beta8[i:i + 1, :].rearrange("o p -> p o"))
        kb.dma("sync", sW, bo_sb[:, i:i + 1], bo8[i:i + 1, :].rearrange("o p -> p o"))
        kb.dma("sync", sW, ub_sb[:, i:i + 1], ub8[i:i + 1, :].rearrange("o p -> p o"))
        kb.dma("sync", sW, vb_sb[:, i:i + 1], vb8[i:i + 1, :].rearrange("o p -> p o"))
    kb.op("vector",
          lambda e: e.tensor_tensor(dvb_sb[:], vb_sb[:], ub_sb[:], ALU.subtract),
          waits=[(sW, sW.n)], incs=[(sV, 1)])
    kb.barrier("consts")

    def _recip_lp(e, out, inp):
        with nc.allow_low_precision(reason="softmax normalizer cast to bf16 (same as prior ScalarE-copy path)"):
            return e.reciprocal(out, inp)

    def ph_on(name):
        return (DEV_PHASES is None or DEV_PHASES == {"ALL"}
                or name in DEV_PHASES)

    if ph_on("R"):
        kb.push()
        relT = kb.buf([128, NCH, 4096], BF16)
        rel_sb = kb.buf([128, 2, D], BF16)
        wr_sb = kb.buf([128, NCH, 1024], BF16)
        rt_out = kb.buf([128, 2, 512], BF16)
        sRL = kb.sem("r_rel")

        for ch in range(NCH):
            kb.dma("gpsimd", sW, wr_sb[:, ch, :], Wr[128 * ch:128 * ch + 128, :])
        wr_done = sW.n

        slot_guard = {0: 0, 1: 0}
        for jt in range(32):
            rows = 128 if jt < 31 else 127
            slot = jt % 2
            kb.dma("gpsimd", sRL, rel_sb[0:rows, slot, :],
                   rel[128 * jt:128 * jt + rows, :],
                   waits=[(sT, slot_guard[slot])])
            ld = sRL.n
            for ch in range(NCH):
                b = ch % 2
                kb.op("tensor",
                      lambda e, rows=rows, slot=slot, ch=ch, b=b:
                      e.transpose(bankb(b)[0:128, 0:rows],
                                  rel_sb[0:rows, slot, 128 * ch:128 * ch + 128],
                                  ident[0:rows, 0:rows]),
                      waits=[(sRL, ld), (sV, sV.n)],
                      incs=[(sT, 1)])
                kb.op("vector",
                      lambda e, rows=rows, jt=jt, ch=ch, b=b:
                      e.tensor_copy(relT[:, ch, 128 * jt:128 * jt + rows],
                                    bankb(b)[0:128, 0:rows]),
                      waits=[(sT, sT.n)], incs=[(sV, 1)])
            slot_guard[slot] = sT.n
        kb.barrier("R-transposed")

        rt_guard = {0: 0, 1: 0}
        for do in range(NCH):
            for jc in range(8):
                b = jc % 2
                for ch in range(NCH):
                    kb.op("tensor",
                          lambda e, do=do, jc=jc, ch=ch, b=b:
                          e.matmul(bank(b), wr_sb[:, ch, 128 * do:128 * do + 128],
                                   relT[:, ch, 512 * jc:512 * jc + 512],
                                   start=(ch == 0), stop=(ch == NCH - 1)),
                          waits=[(sW, wr_done), (sC, sC.n)] if ch == 0 else [],
                          incs=[(sT, 1)] if ch == NCH - 1 else [])
                kb.op("scalar",
                      lambda e, b=b: e.copy(rt_out[:, b, :], bank(b)),
                      waits=[(sT, sT.n), (sS, rt_guard[b])], incs=[(sC, 1)])
                rt_guard[b] = kb.dma(
                    "sync", sS,
                    rt_scr[128 * do:128 * do + 128, 512 * jc:512 * jc + 512],
                    rt_out[:, b, :], waits=[(sC, sC.n)])
        kb.barrier("R-done")
        kb.pop()


    def ln_stats(src_tiles, eps_bias, eps_scale, sq_scr, mrow, rrow, ready=(),
                 ones_src=None):
        if ones_src is None:
            ones_src = onescf
        tiles = [src_tiles(ch) for ch in range(NCH)]
        src_tiles = lambda ch: tiles[ch]
        for ch in range(NCH):
            w = list(ready) if ch == 0 else []
            if ch == 0:
                w += [(sV, sV.n)]
            kb.op("scalar",
                  lambda e, t=tiles[ch]: e.activation(sq_scr[:], t, AF.Square),
                  waits=(list(ready) if ch == 0 else []) + [(sT, sT.n)],
                  incs=[(sC, 1)])
            kb.op("tensor",
                  lambda e, t=tiles[ch], ch=ch: e.matmul(bank(6)[0:1, :], ones_src[:, 0:1], t,
                                            start=(ch == 0), stop=(ch == NCH - 1)),
                  waits=w, incs=[(sT, 1)])
            kb.op("tensor",
                  lambda e, ch=ch: e.matmul(bank(7)[0:1, :], onescf[:, 0:1], sq_scr[:],
                                            start=(ch == 0), stop=(ch == NCH - 1)),
                  waits=[(sC, sC.n)], incs=[(sT, 1)])
        kb.op("vector",
              lambda e: e.tensor_scalar(mrow, bank(6)[0:1, :], 1.0 / D, None, ALU.mult),
              waits=[(sT, sT.n)], incs=[(sV, 1)])
        kb.op("vector",
              lambda e: e.tensor_tensor(rrow, mrow, mrow, ALU.mult),
              waits=[(sV, sV.n)], incs=[(sV, 1)])
        kb.op("vector",
              lambda e: e.scalar_tensor_tensor(rrow, bank(7)[0:1, :], 1.0 / D, rrow,
                                               ALU.mult, ALU.subtract),
              waits=[(sV, sV.n)], incs=[(sV, 1)])
        kb.op("scalar",
              lambda e: e.activation(rrow, rrow, AF.Ln, bias=eps_bias, scale=eps_scale),
              waits=[(sV, sV.n)], incs=[(sC, 1)])
        kb.op("scalar",
              lambda e: e.activation(rrow, rrow, AF.Exp, scale=-0.5),
              waits=[(sC, sC.n)], incs=[(sC, 1)])

    def replicate2(mrow, rrow, mr_sb):
        kb.op("tensor",
              lambda e: e.matmul(bank(4), ones1f[0:1, :], mrow, start=True, stop=True),
              waits=[(sC, sC.n), (sV, sV.n)], incs=[(sT, 1)])
        kb.op("tensor",
              lambda e: e.matmul(bank(5), ones1f[0:1, :], rrow, start=True, stop=True),
              incs=[(sT, 1)])
        t = sT.n
        kb.op("scalar", lambda e: e.copy(mr_sb[:, 0, :], bank(4)),
              waits=[(sT, t), (sV, sV.n)], incs=[(sC, 1)])
        kb.op("scalar", lambda e: e.copy(mr_sb[:, 1, :], bank(5)),
              waits=[(sT, t)], incs=[(sC, 1)])
        return sC.n


    def phase_A1(b, Q1_sb, K_sb, V_tm):
        kb.push()
        xn_sb = kb.buf([128, NCH, L], BF16)
        x_sb = kb.buf([128, NCH, TC], F32)
        sq_scr = kb.buf([128, TC], F32)
        mr_sb = kb.buf([128, 2, TC], BF16)
        w_sb = kb.buf([128, 2, NCH, 128], BF16)
        vtmp = kb.buf([128, TC], BF16)

        for h in range(H):
            kb.op("vector",
                  lambda e, h=h: e.memset(V_tm[:, :, 65 * h + 64: 65 * h + 65], 1.0),
                  incs=[(sV, 1)])

        xg = {"V": 0, "T": 0, "C": 0}
        for tc in range(NTC):
            tsl = slice(TC * tc, TC * tc + TC)
            for ch in range(NCH):
                kb.dma("sync", sX, x_sb[:, ch, :], xT[b, ch, :, tsl],
                       waits=[(sV, xg["V"]), (sT, xg["T"]), (sC, xg["C"])])
            xw = sX.n
            mrow = row_f[0:1, 0, :]
            rrow = row_f[0:1, 1, :]
            ln_stats(lambda ch: x_sb[:, ch, :], EPS1 * EPS2, 1.0 + EPS2,
                     sq_scr, mrow, rrow, ready=[(sX, xw)])
            xg["T"] = sT.n
            xg["C"] = sC.n
            mrd = replicate2(mrow, rrow, mr_sb)
            for ch in range(NCH):
                kb.op("vector",
                      lambda e, ch=ch, tsl=tsl:
                      e.tensor_tensor(xn_sb[:, ch, tsl], x_sb[:, ch, :],
                                      mr_sb[:, 0, :], ALU.subtract),
                      waits=[(sC, mrd)], incs=[(sV, 1)])
                kb.op("vector",
                      lambda e, ch=ch, tsl=tsl:
                      e.tensor_tensor(xn_sb[:, ch, tsl], xn_sb[:, ch, tsl],
                                      mr_sb[:, 1, :], ALU.mult),
                      waits=[(sV, sV.n)], incs=[(sV, 1)])
                kb.op("vector",
                      lambda e, ch=ch, tsl=tsl:
                      e.tensor_scalar(xn_sb[:, ch, tsl], xn_sb[:, ch, tsl],
                                      gam_sb[:, ch:ch + 1], bet_sb[:, ch:ch + 1],
                                      ALU.mult, ALU.add),
                      waits=[(sV, sV.n)], incs=[(sV, 1)])
            xg["V"] = sV.n
            if tc == 0:
                kb.tap("x0", x_sb, [128, NCH, TC], F32)
                kb.tap("mr0", mr_sb, [128, 2, TC], BF16)
        xn_done_V = sV.n
        kb.tap("xn", xn_sb, [128, NCH, L], BF16)

        projs = [("q", Wq), ("k", Wk), ("v", Wv)]
        wslot_guard = {0: 0, 1: 0}
        for pi, (pname, Wap) in enumerate(projs):
            for dc in range(NCH):
                it = pi * NCH + dc
                slot = it % 2
                pb = 4 * (it % 2)
                kb.dma("gpsimd", sWs[slot], w_sb[:, slot, :, :],
                       Wap[:, 128 * dc:128 * dc + 128].rearrange(
                           "(c p) m -> p c m", p=128),
                       waits=[(sT, wslot_guard[slot])])
                first_w = [(sWs[slot], sWs[slot].n), (sV, sV.n), (sC, sC.n)]
                for ch in range(NCH):
                    for tc in range(NTC):
                        tsl = slice(TC * tc, TC * tc + TC)
                        kb.op("tensor",
                              lambda e, ch=ch, tc=tc, slot=slot, pb=pb, tsl=tsl:
                              e.matmul(bank(pb + tc), w_sb[:, slot, ch, :],
                                       xn_sb[:, ch, tsl],
                                       start=(ch == 0), stop=(ch == NCH - 1)),
                              waits=first_w if (ch == 0 and tc == 0) else [],
                              incs=[(sT, 1)] if (ch == NCH - 1 and tc == NTC - 1) else [])
                wslot_guard[slot] = sT.n
                mmd = sT.n
                for tc in range(NTC):
                    tsl = slice(TC * tc, TC * tc + TC)
                    if pname == "q":
                        kb.op("scalar",
                              lambda e, dc=dc, tc=tc, pb=pb, tsl=tsl:
                              e.activation(Q1_sb[:, dc, tsl], bank(pb + tc),
                                           AF.Identity, bias=ub_sb[:, dc:dc + 1]),
                              waits=[(sT, mmd)], incs=[(sC, 1)])
                    elif pname == "k":
                        kb.op("scalar",
                              lambda e, dc=dc, tc=tc, pb=pb, tsl=tsl:
                              e.copy(K_sb[:, dc, tsl], bank(pb + tc)),
                              waits=[(sT, mmd)], incs=[(sC, 1)])
                    else:
                        kb.op("scalar",
                              lambda e, tc=tc, pb=pb:
                              e.copy(vtmp[:], bank(pb + tc)),
                              waits=[(sT, mmd), (sV, sV.n)], incs=[(sC, 1)])
                        evd = sC.n
                        ob = 4 - pb
                        for hh in range(2):
                            hD = 2 * dc + hh
                            for blk in range(4):
                                cblk = tc * 4 + blk
                                tbk = ob + (blk % 2)
                                kb.op("tensor",
                                      lambda e, hh=hh, blk=blk, tbk=tbk:
                                      e.transpose(bankb(tbk)[0:128, 0:64],
                                                  vtmp[64 * hh:64 * hh + 64,
                                                       128 * blk:128 * blk + 128],
                                                  ident[64 * hh:64 * hh + 64,
                                                        64 * hh:64 * hh + 64]),
                                      waits=[(sC, evd), (sV, sV.n)],
                                      incs=[(sT, 1)])
                                kb.op("vector",
                                      lambda e, hD=hD, cblk=cblk, tbk=tbk:
                                      e.tensor_copy(V_tm[:, cblk, 65 * hD:65 * hD + 64],
                                                    bankb(tbk)[0:128, 0:64]),
                                      waits=[(sT, sT.n)],
                                      incs=[(sV, 1)])
        kb.tap("q1", Q1_sb, [128, NCH, L], BF16)
        kb.tap("k", K_sb, [128, NCH, L], BF16)
        kb.tap("vtm", V_tm, [128, NLT, H * 65], BF16)
        kb.barrier("A1-done")
        kb.pop()


    def phase_A2(b, Q1_sb, K_sb, V_tm, Ofm):
        kb.push()
        NS_BD, NS_S, NS_Z, NS_P = 2, 3, 3, 2
        rt_sb = kb.buf([128, 2, 4096], BF16)
        BDsb, bd_off = kb.buf2([128, NS_BD, 2176], BF16)
        Ssb = kb.buf([128, NS_S, L], BF16)
        Zsb = kb.buf([128, NS_Z, L], BF16)
        PT = kb.buf([128, NS_P, L], BF16)
        q2t = kb.buf([128, 2, 128], BF16)
        negmax = kb.buf([128, 4], F32)
        rr = kb.buf([1, 2, 128], F32)
        rp = kb.buf([64, 2, 128], BF16)
        sRT = kb.sems.get("rtld") or kb.sem("rtld")

        PVo = psA[0:65, 3072:3200]
        PVrow = psA[64:65, 3072:3200]
        RPp = psA[0:64, 3200:3328]

        gd = {
            "bd_ev": [0, 0],
            "ac_ev": [0, 0],
            "bdsb_D": [0] * NS_BD,
            "ssb_V": [0] * NS_S, "ssb_D": [0] * NS_S,
            "zsb_T": [0] * NS_Z,
            "zt_C": [0, 0],
            "pt_T": [0] * NS_P,
            "b6_V": 0, "b6_C": 0, "rp_V": 0,
            "rt_T": [0, 0],
            "q2_T": [0, 0],
            "diag_D": [0] * 999,
            "acev_C": [0] * 999,
            "bdev_V": [0] * 999,
            "sub_V": [0] * 999,
            "exp_C": [0] * 999, "exp1_C": [0] * 999,
        }
        def allw():
            return [(sT, sT.n), (sV, sV.n), (sC, sC.n), (sD, sD.n)]

        iters = []
        for hp in range(8):
            for hh in range(2):
                for lt in range(NLT):
                    iters.append((hp, hh, lt))
        if DEV_A2N:
            iters = iters[:DEV_A2N]
        NIT = len(iters)
        rtld_n = {}

        def S0(i):
            hp, hh, lt = iters[i]
            if hh == 0 and lt == 0:
                for target in ([hp, hp + 1] if hp == 0 else [hp + 1]):
                    if target < 8:
                        slot = target % 2
                        kb.dma("sync", sRTs[slot], rt_sb[:, slot, :],
                               rt_scr[128 * target:128 * target + 128, 0:4096],
                               waits=[(sT, sT.n)])
                        rtld_n[target] = (sRTs[slot], sRTs[slot].n)
            pi = i % 2
            hb = 64 * hh
            hsl = slice(hb, hb + 64)
            lsl = slice(128 * lt, 128 * lt + 128)
            j0 = 1920 - 128 * lt
            kb.op("vector",
                  lambda e, hsl=hsl, pi=pi, hp=hp, lsl=lsl:
                  e.tensor_scalar(q2t[hsl, pi, :], Q1_sb[hsl, hp, lsl],
                                  dvb_sb[hsl, hp:hp + 1], None, ALU.add),
                  waits=[(sT, gd["q2_T"][pi])], incs=[(sV, 1)])
            q2v = sV.n
            sb = i % NS_BD
            for jc in range(5):
                n = 512 if jc < 4 else 128
                bki = jc % 2
                w = [(sV, gd["bd_ev"][bki])]
                if jc == 0:
                    w += [(sV, q2v), rtld_n[hp]]
                kb.op("tensor",
                      lambda e, jc=jc, n=n, bki=bki, hsl=hsl, pi=pi,
                      slot=hp % 2, j0=j0:
                      e.matmul(psA[:, 512 * bki:512 * bki + n],
                               q2t[hsl, pi, :],
                               rt_sb[hsl, slot, j0 + 512 * jc:j0 + 512 * jc + n],
                               start=True, stop=True),
                      waits=w, incs=[(sT, 1)])
                wv = [(sT, sT.n)]
                if jc == 0 and gd["bdsb_D"][sb] != 0:
                    wv.append(gd["bdsb_D"][sb])
                kb.op("vector",
                      lambda e, jc=jc, n=n, bki=bki, sb=sb:
                      e.tensor_copy(BDsb[:, sb, 512 * jc:512 * jc + n],
                                    psA[:, 512 * bki:512 * bki + n]),
                      waits=wv, incs=[(sV, 1)])
                gd["bd_ev"][bki] = sV.n
            gd["q2_T"][pi] = sT.n
            gd["bdev_V"][i] = sV.n
            ss = i % NS_S
            for c in range(4):
                bki = c % 2
                w = [(sC, gd["ac_ev"][bki])]
                kb.op("tensor",
                      lambda e, c=c, bki=bki, hsl=hsl, hp=hp, lsl=lsl:
                      e.matmul(psA[:, 1024 + 512 * bki:1536 + 512 * bki],
                               Q1_sb[hsl, hp, lsl],
                               K_sb[hsl, hp, 512 * c:512 * c + 512],
                               start=True, stop=True),
                      waits=w, incs=[(sT, 1)])
                wv = [(sT, sT.n)]
                if c == 0:
                    wv += [(sV, gd["ssb_V"][ss])]
                    if gd["ssb_D"][ss] != 0:
                        wv.append(gd["ssb_D"][ss])
                kb.op("scalar",
                      lambda e, c=c, bki=bki, ss=ss:
                      e.copy(Ssb[:, ss, 512 * c:512 * c + 512],
                             psA[:, 1024 + 512 * bki:1536 + 512 * bki]),
                      waits=wv, incs=[(sC, 1)])
                gd["ac_ev"][bki] = sC.n
            gd["acev_C"][i] = sC.n

        def S1(i):
            sb = i % NS_BD
            ss = i % NS_S
            dsem = sDs[i % 3]
            kb.op("gpsimd",
                  lambda e, ss=ss, sb=sb:
                  e.dma_start(out=Ssb[:, ss, :],
                              in_=kb.diag_ap(BF16, bd_off + sb * 2176, 128, L),
                              accum_op=ALU.add),
                  waits=[(sC, gd["acev_C"][i]), (sV, gd["bdev_V"][i])],
                  incs=[(dsem, 16)], is_dma=True)
            kb.dma_sems.add(dsem)
            gd["bdsb_D"][sb] = (dsem, dsem.n)
            gd["diag_D"][i] = (dsem, dsem.n)

        def S2(i):
            ss = i % NS_S
            sz = i % NS_Z
            nm = i % 4
            kb.op("vector",
                  lambda e, ss=ss, nm=nm:
                  e.tensor_reduce(negmax[:, nm:nm + 1], Ssb[:, ss, :],
                                  AX.X, ALU.max, negate=True),
                  waits=[gd["diag_D"][i]], incs=[(sV, 1)])

            kb.op("vector",
                  lambda e, ss=ss, sz=sz, nm=nm:
                  e.tensor_scalar(Zsb[:, sz, :], Ssb[:, ss, :],
                                  negmax[:, nm:nm + 1], None, ALU.add),
                  waits=[(sV, sV.n), (sT, gd["zsb_T"][sz])],
                  incs=[(sV, 1)])
            gd["ssb_V"][ss] = sV.n
            gd["ssb_D"][ss] = gd["diag_D"][i]
            gd["sub_V"][i] = sV.n

        def S3(i):
            sz = i % NS_Z
            sp = i % NS_P
            for k in range(16):
                tb = 4 + k // 8
                w = []
                if k == 0:
                    w = [(sV, gd["sub_V"][i]), (sC, gd["zt_C"][0])]
                elif k == 8:
                    w = [(sC, gd["zt_C"][1])]
                kb.op("tensor",
                      lambda e, k=k, tb=tb, sz=sz:
                      e.transpose(bankb(tb)[:, 128 * (k % 8):128 * (k % 8) + 128],
                                  Zsb[:, sz, 128 * k:128 * k + 128],
                                  ident[:]),
                      waits=w, incs=[(sT, 1)] if k in (7, 15) else [])
                if k == 7:
                    t1 = sT.n
            t2 = sT.n
            gd["zsb_T"][sz] = sT.n
            kb.op("scalar",
                  lambda e, sp=sp:
                  e.activation(PT[:, sp, 0:1024], bankb(4), AF.Exp, scale=0.125),
                  waits=[(sT, t1), (sT, gd["pt_T"][sp])], incs=[(sC, 1)])
            gd["zt_C"][0] = sC.n
            gd["exp1_C"][i] = sC.n
            kb.op("scalar",
                  lambda e, sp=sp:
                  e.activation(PT[:, sp, 1024:2048], bankb(5), AF.Exp, scale=0.125),
                  waits=[(sT, t2)], incs=[(sC, 1)])
            gd["zt_C"][1] = sC.n
            gd["exp_C"][i] = sC.n

        def S4(i):
            hp, hh, lt = iters[i]
            h = 2 * hp + hh
            hb = 64 * hh
            hsl = slice(hb, hb + 64)
            lsl = slice(128 * lt, 128 * lt + 128)
            sp = i % NS_P
            pi = i % 2
            for mc in range(16):
                w = []
                if mc == 0:
                    w = [(sC, gd["exp1_C"][i]), (sV, gd["b6_V"]), (sC, gd["b6_C"])]
                elif mc == 8:
                    w = [(sC, gd["exp_C"][i])]
                kb.op("tensor",
                      lambda e, mc=mc, h=h, sp=sp:
                      e.matmul(PVo, V_tm[:, mc, 65 * h:65 * h + 65],
                               PT[:, sp, 128 * mc:128 * mc + 128],
                               start=(mc == 0), stop=(mc == 15)),
                      waits=w, incs=[(sT, 1)] if mc == 15 else [])
            gd["pt_T"][sp] = sT.n
            pvd = sT.n
            kb.op("vector",
                  lambda e, pi=pi: e.reciprocal(rr[0:1, pi, :], PVrow),
                  waits=[(sT, pvd)], incs=[(sV, 1)])
            kb.op("tensor",
                  lambda e, pi=pi:
                  e.matmul(RPp, ones1f[0:1, 0:64], rr[0:1, pi, :],
                           start=True, stop=True),
                  waits=[(sV, sV.n), (sC, gd["b6_C"])], incs=[(sT, 1)])
            kb.op("scalar",
                  lambda e, pi=pi: e.copy(rp[:, pi, :], RPp),
                  waits=[(sT, sT.n)], incs=[(sC, 1)])
            gd["b6_C"] = sC.n
            kb.op("vector",
                  lambda e, hsl=hsl, hp=hp, lsl=lsl, pi=pi:
                  e.tensor_tensor(Ofm[hsl, hp, lsl], psA[0:64, 3072:3200],
                                  rp[:, pi, :], ALU.mult),
                  waits=[(sC, sC.n)], incs=[(sV, 1)])
            gd["b6_V"] = sV.n

        for cyc in range(NIT + 4):
            if cyc < NIT:
                S0(cyc)
            if 1 <= cyc < NIT + 1:
                S1(cyc - 1)
            if 2 <= cyc < NIT + 2:
                S2(cyc - 2)
            if 3 <= cyc < NIT + 3:
                S3(cyc - 3)
            if 4 <= cyc < NIT + 4:
                S4(cyc - 4)

        kb.tap("ofm", Ofm, [128, NCH, L], BF16)
        kb.barrier("A2-done")
        if b == 0:
            li = NIT - 1
            kb.tap("sdump", Ssb[:, li % NS_S, :], [128, L], BF16)
            kb.tap("bddump", BDsb[:, li % NS_BD, :], [128, 2304], BF16)
            kb.tap("ptdump", PT[:, li % NS_P, :], [128, L], BF16)
            kb.tap("rtdump", rt_sb[:, 0, :], [128, 4096], BF16)
            kb.tap("q2dump", q2t[:, li % 2, :], [128, 128], BF16)
        kb.barrier("A2-dumps")
        kb.pop()


    def phase_A3(b, Ofm, z_sb):
        kb.push()
        wo_sb = kb.buf([128, NCH, D], BF16)
        a_sb = kb.buf([128, NCH, TC], BF16)
        x_sb = kb.buf([128, NCH, TC], F32)
        x1c = kb.buf([128, NCH, TC], F32)
        sq_scr = kb.buf([128, TC], F32)
        mr_sb = kb.buf([128, 2, TC], BF16)
        mr2_sb = kb.buf([128, 2, TC], BF16)

        kb.dma("gpsimd", sW, wo_sb[:, :, :],
               Wo.rearrange("(c p) m -> p c m", p=128))
        for ch in range(NCH):
            kb.op("vector",
                  lambda e, ch=ch: e.memset(z_sb[:, ch, 0:1], 0.0),
                  incs=[(sV, 1)])
        wod = sW.n

        bank_guard = [0, 0, 0, 0]
        xg = {"V": 0, "T": 0, "C": 0}
        for tc in range(NTC):
            tsl = slice(TC * tc, TC * tc + TC)
            for ch in range(NCH):
                kb.dma("sync", sX, x_sb[:, ch, :], xT[b, ch, :, tsl],
                       waits=[(sV, xg["V"])])
            xw = sX.n
            for dc in range(NCH):
                bk = dc % 4
                for ic in range(NCH):
                    w = []
                    if ic == 0:
                        w = [(sC, bank_guard[bk])]
                        if dc == 0 and tc == 0:
                            w.append((sW, wod))
                    kb.op("tensor",
                          lambda e, dc=dc, ic=ic, bk=bk, tsl=tsl:
                          e.matmul(bank(bk), wo_sb[:, ic, 128 * dc:128 * dc + 128],
                                   Ofm[:, ic, tsl],
                                   start=(ic == 0), stop=(ic == NCH - 1)),
                          waits=w, incs=[(sT, 1)] if ic == NCH - 1 else [])
                kb.op("scalar",
                      lambda e, dc=dc, bk=bk:
                      e.activation(a_sb[:, dc, :], bank(bk), AF.Identity,
                                   bias=bo_sb[:, dc:dc + 1]),
                      waits=[(sT, sT.n)] + ([(sV, xg["V"])] if dc == 0 else []),
                      incs=[(sC, 1)])
                bank_guard[bk] = sC.n
            mrow = row_f[0:1, 0, :]
            rrow = row_f[0:1, 1, :]
            ln_stats(lambda ch: a_sb[:, ch, :], EPS1, 1.0, sq_scr, mrow, rrow,
                     ready=[(sC, sC.n)], ones_src=onescb)
            replicate2(mrow, rrow, mr_sb)
            mrd = sC.n
            for ch in range(NCH):
                kb.op("vector",
                      lambda e, ch=ch:
                      e.tensor_tensor(a_sb[:, ch, :], a_sb[:, ch, :],
                                      mr_sb[:, 0, :], ALU.subtract),
                      waits=[(sC, mrd), (sX, xw)], incs=[(sV, 1)])
                kb.op("vector",
                      lambda e, ch=ch:
                      e.tensor_tensor(a_sb[:, ch, :], a_sb[:, ch, :],
                                      mr_sb[:, 1, :], ALU.mult),
                      waits=[(sV, sV.n)], incs=[(sV, 1)])
                kb.op("vector",
                      lambda e, ch=ch:
                      e.tensor_tensor(x1c[:, ch, :], x_sb[:, ch, :],
                                      a_sb[:, ch, :], ALU.add),
                      waits=[(sV, sV.n)], incs=[(sV, 1)])
                kb.dma("sync", sO, x1_park[ch, :, tsl], x1c[:, ch, :],
                       waits=[(sV, sV.n)])
            mrow2 = row_f[0:1, 2, :]
            rrow2 = row_f[0:1, 3, :]
            ln_stats(lambda ch: x1c[:, ch, :], EPS1, 1.0, sq_scr, mrow2, rrow2,
                     ready=[(sV, sV.n)])
            replicate2(mrow2, rrow2, mr2_sb)
            m2d = sC.n
            for ch in range(NCH):
                kb.op("vector",
                      lambda e, ch=ch, tc=tc:
                      e.tensor_tensor(z_sb[:, ch, 1 + TC * tc:1 + TC * tc + TC],
                                      x1c[:, ch, :], mr2_sb[:, 0, :], ALU.subtract),
                      waits=[(sC, m2d)], incs=[(sV, 1)])
                kb.op("vector",
                      lambda e, ch=ch, tc=tc:
                      e.tensor_tensor(z_sb[:, ch, 1 + TC * tc:1 + TC * tc + TC],
                                      z_sb[:, ch, 1 + TC * tc:1 + TC * tc + TC],
                                      mr2_sb[:, 1, :], ALU.mult),
                      waits=[(sV, sV.n)], incs=[(sV, 1)])
            xg["V"] = sV.n
        kb.tap("x1", x1_park, [NCH, 128, L], F32)
        kb.tap("z", z_sb, [128, NCH, 1 + L], F32)
        kb.barrier("A3-done")
        kb.pop()


    def phase_F(b, l, z_sb, rotm_sb, bd8_sb, oneh_sb):
        PADL = FW + L
        kb.push()
        Qf = kb.buf([128, 4, L], BF16, tag=("qkvf", b, l))
        Kf = kb.buf([128, 4, PADL], BF16, tag=("qkvf", b, l))
        Vtm = kb.buf([128, 20, FH * 65], BF16, tag=("qkvf", b, l))
        qs_sb = kb.buf([128, 2], F32)
        lgb_sb = kb.buf([128, 2, NCH], F32)
        b1_sb = kb.buf([128, 2, NH1], F32)
        b2_sb = kb.buf([128, NCH], F32)
        kb.dma("sync", sW, qs_sb[:, 0:1], f_qs[l:l + 1, :].rearrange("o p -> p o"))
        kb.dma("sync", sW, qs_sb[:, 1:2], f_ks[l:l + 1, :].rearrange("o p -> p o"))
        for ch in range(NCH):
            kb.dma("sync", sW, lgb_sb[:, 0, ch:ch + 1],
                   f_lng[l, ch:ch + 1, :].rearrange("o p -> p o"))
            kb.dma("sync", sW, lgb_sb[:, 1, ch:ch + 1],
                   f_lnb[l, ch:ch + 1, :].rearrange("o p -> p o"))
            kb.dma("sync", sW, b2_sb[:, ch:ch + 1],
                   fw_b2[l, ch:ch + 1, :].rearrange("o p -> p o"))
        for gg in range(2):
            for kk in range(NH1):
                kb.dma("sync", sW, b1_sb[:, gg, kk:kk + 1],
                       fw_b1[l, gg, kk:kk + 1, :].rearrange("o p -> p o"))
        smalls_w = sW.n

        kb.push()
        xfb = kb.buf([128, NCH, 1 + L], BF16)
        wq_sb = kb.buf([128, 2, NCH, 128], BF16)
        vtmp = kb.buf([128, TC], BF16)
        for ch in range(NCH):
            kb.op("vector",
                  lambda e, ch=ch: e.tensor_copy(xfb[:, ch, :], z_sb[:, ch, :]),
                  incs=[(sV, 1)])
        xfbd = sV.n
        for c4 in range(4):
            kb.op("vector", lambda e, c4=c4: e.memset(Kf[:, c4, 0:FW], 0.0),
                  incs=[(sV, 1)])
        for cb in range(4):
            kb.op("vector", lambda e, cb=cb: e.memset(Vtm[:, cb, :], 0.0),
                  incs=[(sV, 1)])
        for h in range(FH):
            kb.op("vector",
                  lambda e, h=h: e.memset(Vtm[:, 4:20, 65 * h + 64:65 * h + 65], 1.0),
                  incs=[(sV, 1)])
        pads_v = sV.n

        def shifted(ch, tsl_lo):
            off = 0 if ch >= 4 else 1
            return xfb[:, ch, off + tsl_lo: off + tsl_lo + TC]

        wslot = {0: 0, 1: 0}
        for dc in range(12):
            slot = dc % 2
            pb = 4 * slot
            kb.dma("gpsimd", sWs[slot], wq_sb[:, slot, :, :],
                   fw_qkv[l, :, 128 * dc:128 * dc + 128].rearrange(
                       "(c p) m -> p c m", p=128),
                   waits=[(sT, wslot[slot])])
            first_w = [(sWs[slot], sWs[slot].n), (sV, sV.n), (sC, sC.n)]
            for ch in range(NCH):
                for tc in range(NTC):
                    kb.op("tensor",
                          lambda e, ch=ch, tc=tc, slot=slot, pb=pb:
                          e.matmul(bank(pb + tc), wq_sb[:, slot, ch, :],
                                   shifted(ch, TC * tc),
                                   start=(ch == 0), stop=(ch == NCH - 1)),
                          waits=first_w if (ch == 0 and tc == 0) else [],
                          incs=[(sT, 1)] if (ch == NCH - 1 and tc == NTC - 1) else [])
            wslot[slot] = sT.n
            mmd = sT.n
            for tc in range(NTC):
                tsl = slice(TC * tc, TC * tc + TC)
                if dc < 4:
                    kb.op("scalar",
                          lambda e, dc=dc, tc=tc, pb=pb, tsl=tsl:
                          e.copy(Qf[:, dc, tsl], bank(pb + tc)),
                          waits=[(sT, mmd)], incs=[(sC, 1)])
                elif dc < 8:
                    kb.op("scalar",
                          lambda e, dc=dc, tc=tc, pb=pb, tsl=tsl:
                          e.copy(Kf[:, dc - 4, FW + TC * tc:FW + TC * tc + TC],
                                 bank(pb + tc)),
                          waits=[(sT, mmd)], incs=[(sC, 1)])
                else:
                    kb.op("scalar",
                          lambda e, tc=tc, pb=pb: e.copy(vtmp[:], bank(pb + tc)),
                          waits=[(sT, mmd), (sV, sV.n)], incs=[(sC, 1)])
                    evd = sC.n
                    ob = 4 - pb
                    for hh in range(2):
                        hD = 2 * (dc - 8) + hh
                        for blk in range(4):
                            cblk = 4 + tc * 4 + blk
                            tbk = ob + (blk % 2)
                            kb.op("tensor",
                                  lambda e, hh=hh, blk=blk, tbk=tbk:
                                  e.transpose(bankb(tbk)[0:128, 0:64],
                                              vtmp[64 * hh:64 * hh + 64,
                                                   128 * blk:128 * blk + 128],
                                              ident[64 * hh:64 * hh + 64,
                                                    64 * hh:64 * hh + 64]),
                                  waits=[(sC, evd), (sV, max(sV.n, pads_v))],
                                  incs=[(sT, 1)])
                            kb.op("vector",
                                  lambda e, hD=hD, cblk=cblk, tbk=tbk:
                                  e.tensor_copy(Vtm[:, cblk, 65 * hD:65 * hD + 64],
                                                bankb(tbk)[0:128, 0:64]),
                                  waits=[(sT, sT.n)], incs=[(sV, 1)])
        kb.pop()
        Ofml = kb.buf([128, 4, L], BF16, tag=("ofml", b, l))
        kb.barrier(f"F1-{b}-{l}")

        kb.push()
        rot_sb = kb.buf([128, 4, L], BF16)
        for j, cr in enumerate((c_rotqc, c_rotqs, c_rotkc, c_rotks)):
            kb.dma("gpsimd", sW, rot_sb[:, j, :], cr[:])
        kb.barrier(f"F2tab-{b}-{l}")
        sqf = kb.buf([128, TC], BF16)
        rn_sb = kb.buf([8, 2, TC], F32)
        rtmp = kb.buf([128, TC], BF16)
        for qk in range(2):
            tab_c = rot_sb[:, 2 * qk, :]
            tab_s = rot_sb[:, 2 * qk + 1, :]
            for tc in range(NTC):
                tsl = slice(TC * tc, TC * tc + TC)

                def src(c4, qk=qk, tsl=tsl):
                    if qk == 0:
                        return Qf[:, c4, tsl]
                    return Kf[:, c4, FW + TC * tc:FW + TC * tc + TC]

                for c4 in range(4):
                    kb.op("scalar",
                          lambda e, t=src(c4): e.activation(sqf[:], t, AF.Square),
                          waits=[(sT, sT.n)], incs=[(sC, 1)])
                    kb.op("tensor",
                          lambda e, c4=c4:
                          e.matmul(bank(6)[0:8, :], bd8_sb[:, c4, :], sqf[:],
                                   start=(c4 == 0), stop=(c4 == 3)),
                          waits=[(sC, sC.n)] + ([(sV, sV.n)] if c4 == 0 else []),
                          incs=[(sT, 1)])
                kb.op("scalar",
                      lambda e, qk=qk: e.activation(rn_sb[0:8, qk, :], bank(6)[0:8, :],
                                                    AF.Ln),
                      waits=[(sT, sT.n)], incs=[(sC, 1)])
                kb.op("scalar",
                      lambda e, qk=qk: e.activation(rn_sb[0:8, qk, :], rn_sb[0:8, qk, :],
                                                    AF.Exp, scale=-0.5),
                      waits=[(sC, sC.n)], incs=[(sC, 1)])
                kb.op("vector",
                      lambda e, qk=qk: e.tensor_scalar(rn_sb[0:8, qk, :],
                                                       rn_sb[0:8, qk, :],
                                                       1e12, None, ALU.min),
                      waits=[(sC, sC.n)], incs=[(sV, 1)])
                rnd = sV.n
                for c4 in range(4):
                    bk = c4 % 2
                    kb.op("tensor",
                          lambda e, c4=c4, bk=bk, qk=qk:
                          e.matmul(bank(bk), oneh_sb[0:8, c4, :], rn_sb[0:8, qk, :],
                                   start=True, stop=True),
                          waits=[(sV, rnd) if c4 == 0 else (sV, sV.n)],
                          incs=[(sT, 1)])
                    kb.op("vector",
                          lambda e, t=src(c4), bk=bk, qk=qk:
                          e.scalar_tensor_tensor(t, t,
                                                 qs_sb[:, qk:qk + 1], bank(bk),
                                                 ALU.mult, ALU.mult),
                          waits=[(sT, sT.n), (sW, smalls_w)], incs=[(sV, 1)])
                    kb.op("tensor",
                          lambda e, t=src(c4), bk=bk:
                          e.matmul(bank(2 + bk), rotm_sb[:], t,
                                   start=True, stop=True),
                          waits=[(sV, sV.n)], incs=[(sT, 1)])
                    kb.op("vector",
                          lambda e, c4=c4, bk=bk, tab_s=tab_s, tsl=tsl:
                          e.tensor_tensor(rtmp[:], bank(2 + bk), tab_s[:, tsl],
                                          ALU.mult),
                          waits=[(sT, sT.n)], incs=[(sV, 1)])
                    kb.op("vector",
                          lambda e, t=src(c4), tab_c=tab_c, tsl=tsl:
                          e.tensor_tensor(t, t, tab_c[:, tsl], ALU.mult),
                          waits=[(sV, sV.n)], incs=[(sV, 1)])
                    kb.op("vector",
                          lambda e, t=src(c4):
                          e.tensor_tensor(t, t, rtmp[:], ALU.add),
                          waits=[(sV, sV.n)], incs=[(sV, 1)])
        kb.pop()
        kb.barrier(f"F2-{b}-{l}")
        kb.tap(f"qrot{l}", Qf, [128, 4, L], BF16)
        kb.tap(f"krot{l}", Kf, [128, 4, PADL], BF16)

        kb.push()
        mk_sb = kb.buf([128, 8, 1024], BF16)
        for mi in range(8):
            kb.dma("gpsimd", sW, mk_sb[:, mi, :], c_masks[mi])
        kb.barrier(f"F3tab-{b}-{l}")
        NSF = 3
        Ssl = kb.buf([128, NSF, 1024], BF16)
        Zsl = kb.buf([128, NSF, 1024], BF16)
        PTl = kb.buf([128, NSF, 1024], BF16)
        negm = kb.buf([128, 4], F32)
        rrl = kb.buf([1, 2, 128], F32)
        rpl = kb.buf([64, 2, 128], BF16)
        PVl = psA[0:65, 3072:3200]
        PVlrow = psA[64:65, 3072:3200]
        RPl = psA[0:64, 3200:3328]
        g3 = {"sc_ev": [0, 0], "zt_C": [0, 0], "ss_V": [0] * NSF,
              "zs_T": [0] * NSF, "pt_T": [0] * NSF,
              "b6_V": 0, "b6_C": 0, "rp_V": 0,
              "mm_T": [0] * 999, "sub_V": [0] * 999, "exp_C": [0] * 999}
        tiles = []
        for h in range(FH):
            for w in range(NW):
                for lt in range(4):
                    tiles.append((h, w, lt))
        NTI = len(tiles)

        def T0(i):
            h, w, lt = tiles[i]
            hb = 64 * (h % 2)
            hsl = slice(hb, hb + 64)
            c4 = h // 2
            qsl = slice(FW * w + 128 * lt, FW * w + 128 * lt + 128)
            sb0 = 4 * (i % 2)
            for half in range(2):
                ww = []
                if half == 0:
                    ww = [(sV, g3["sc_ev"][i % 2])]
                kb.op("tensor",
                      lambda e, half=half, hsl=hsl, c4=c4, w=w, qsl=qsl, sb0=sb0:
                      e.matmul(psA[:, 512 * (sb0 + half):512 * (sb0 + half) + 512],
                               Qf[hsl, c4, qsl],
                               Kf[hsl, c4, FW * w + 512 * half:
                                  FW * w + 512 * half + 512],
                               start=True, stop=True),
                      waits=ww, incs=[(sT, 1)] if half == 1 else [])
            g3["mm_T"][i] = sT.n

        def T1(i):
            h, w, lt = tiles[i]
            sb0 = 4 * (i % 2)
            sf = i % NSF
            nm = i % 4
            mi = (4 if w == 0 else 0) + lt
            kb.op("vector",
                  lambda e, sf=sf, mi=mi, sb0=sb0:
                  e.tensor_tensor(Ssl[:, sf, :], psA[:, 512 * sb0:512 * sb0 + 1024],
                                  mk_sb[:, mi, :], ALU.add),
                  waits=[(sT, g3["mm_T"][i]), (sV, g3["ss_V"][sf])],
                  incs=[(sV, 1)])
            g3["sc_ev"][i % 2] = sV.n
            kb.op("vector",
                  lambda e, sf=sf, nm=nm:
                  e.tensor_reduce(negm[:, nm:nm + 1], Ssl[:, sf, :],
                                  AX.X, ALU.max, negate=True),
                  waits=[(sV, sV.n)], incs=[(sV, 1)])
            kb.op("vector",
                  lambda e, sf=sf, nm=nm:
                  e.tensor_scalar(Zsl[:, sf, :], Ssl[:, sf, :],
                                  negm[:, nm:nm + 1], None, ALU.add),
                  waits=[(sV, sV.n), (sT, g3["zs_T"][sf])],
                  incs=[(sV, 1)])
            g3["ss_V"][sf] = sV.n
            g3["sub_V"][i] = sV.n

        def T2(i):
            sf = i % NSF
            ztb = 2 + (i % 2)
            for k in range(8):
                ww = []
                if k == 0:
                    ww = [(sV, g3["sub_V"][i]), (sC, g3["zt_C"][i % 2])]
                kb.op("tensor",
                      lambda e, k=k, sf=sf, ztb=ztb:
                      e.transpose(bankb(ztb)[:, 128 * k:128 * k + 128],
                                  Zsl[:, sf, 128 * k:128 * k + 128],
                                  ident[:]),
                      waits=ww, incs=[(sT, 1)] if k == 7 else [])
            g3["zs_T"][sf] = sT.n
            kb.op("scalar",
                  lambda e, sf=sf, ztb=ztb:
                  e.activation(PTl[:, sf, :], bankb(ztb), AF.Exp),
                  waits=[(sT, sT.n), (sT, g3["pt_T"][sf])], incs=[(sC, 1)])
            g3["zt_C"][i % 2] = sC.n
            g3["exp_C"][i] = sC.n

        def T3(i):
            h, w, lt = tiles[i]
            hb = 64 * (h % 2)
            hsl = slice(hb, hb + 64)
            c4 = h // 2
            qsl = slice(FW * w + 128 * lt, FW * w + 128 * lt + 128)
            sf = i % NSF
            pi = i % 2
            for mc in range(8):
                ww = []
                if mc == 0:
                    ww = [(sC, g3["exp_C"][i]), (sV, g3["b6_V"]), (sC, g3["b6_C"])]
                kb.op("tensor",
                      lambda e, mc=mc, h=h, w=w, sf=sf:
                      e.matmul(PVl, Vtm[:, 4 * w + mc, 65 * h:65 * h + 65],
                               PTl[:, sf, 128 * mc:128 * mc + 128],
                               start=(mc == 0), stop=(mc == 7)),
                      waits=ww, incs=[(sT, 1)] if mc == 7 else [])
            g3["pt_T"][sf] = sT.n
            kb.op("vector",
                  lambda e, pi=pi: e.reciprocal(rrl[0:1, pi, :], PVlrow),
                  waits=[(sT, sT.n)], incs=[(sV, 1)])
            kb.op("tensor",
                  lambda e, pi=pi:
                  e.matmul(RPl, ones1f[0:1, 0:64], rrl[0:1, pi, :],
                           start=True, stop=True),
                  waits=[(sV, sV.n), (sC, g3["b6_C"])], incs=[(sT, 1)])
            kb.op("scalar",
                  lambda e, pi=pi: e.copy(rpl[:, pi, :], RPl),
                  waits=[(sT, sT.n)], incs=[(sC, 1)])
            g3["b6_C"] = sC.n
            kb.op("vector",
                  lambda e, hsl=hsl, c4=c4, qsl=qsl, pi=pi:
                  e.tensor_tensor(Ofml[hsl, c4, qsl],
                                  psA[0:64, 3072:3200],
                                  rpl[:, pi, :], ALU.mult),
                  waits=[(sC, sC.n)], incs=[(sV, 1)])
            g3["b6_V"] = sV.n

        for cyc in range(NTI + 3):
            if cyc < NTI:
                T0(cyc)
            if 1 <= cyc < NTI + 1:
                T1(cyc - 1)
            if 2 <= cyc < NTI + 2:
                T2(cyc - 2)
            if 3 <= cyc < NTI + 3:
                T3(cyc - 3)
        kb.pop()
        kb.barrier(f"F3-{b}-{l}")
        kb.tap(f"ol{l}", Ofml, [128, 4, L], BF16)
        kb.free_tag(("qkvf", b, l))

        kb.push()
        fwo_sb = kb.buf([128, 4, D], BF16)
        for ic in range(4):
            kb.dma("gpsimd", sW, fwo_sb[:, ic, :],
                   fw_wo[l, 128 * ic:128 * ic + 128, :])
        fwod = sW.n
        bank_guard4 = [0, 0, 0, 0]
        for tc in range(NTC):
            tsl = slice(TC * tc, TC * tc + TC)
            for dc in range(NCH):
                bk = dc % 4
                for ic in range(4):
                    ww = []
                    if ic == 0:
                        ww = [(sV, bank_guard4[bk])]
                        if dc == 0 and tc == 0:
                            ww.append((sW, fwod))
                    kb.op("tensor",
                          lambda e, dc=dc, ic=ic, bk=bk, tsl=tsl:
                          e.matmul(bank(bk), fwo_sb[:, ic, 128 * dc:128 * dc + 128],
                                   Ofml[:, ic, tsl],
                                   start=(ic == 0), stop=(ic == 3)),
                          waits=ww, incs=[(sT, 1)] if ic == 3 else [])
                kb.op("vector",
                      lambda e, dc=dc, bk=bk, tc=tc:
                      e.tensor_tensor(z_sb[:, dc, 1 + TC * tc:1 + TC * tc + TC],
                                      psA[:, 512 * bk:512 * bk + 512],
                                      z_sb[:, dc, 1 + TC * tc:1 + TC * tc + TC],
                                      ALU.add),
                      waits=[(sT, sT.n)], incs=[(sV, 1)])
                bank_guard4[bk] = sV.n
        kb.pop()
        kb.barrier(f"F4-{b}-{l}")
        kb.tap(f"mid{l}", z_sb, [128, NCH, 1 + L], F32)
        kb.free_tag(("ofml", b, l))

        kb.push()
        h_sb = kb.buf([128, NH1, L // 2], BF16)
        hn_sb = kb.buf([128, NCH, TC], BF16)
        sq5 = kb.buf([128, TC], F32)
        mr5 = kb.buf([128, 2, TC], BF16)
        glt = kb.buf([128, 2, TC], BF16)
        w1_sb = kb.buf([128, 2, NCH * 256], BF16)
        w2_sb = kb.buf([128, 2, NH1 * 128], BF16)

        def shifted_z(ch, lo):
            off = 0 if ch >= 4 else 1
            return z_sb[:, ch, off + lo: off + lo + TC]

        w1slot = {0: 0, 1: 0}
        w2slot = {0: 0, 1: 0}
        for half in range(2):
            for tc in (2 * half, 2 * half + 1):
                tci = tc - 2 * half
                tlo = TC * tc
                mrow = row_f[0:1, 0, :]
                rrow = row_f[0:1, 1, :]
                ln_stats(lambda ch: shifted_z(ch, tlo), EPS1, 1.0, sq5, mrow, rrow,
                         ready=[(sV, sV.n)])
                replicate2(mrow, rrow, mr5)
                mrd = sC.n
                for ch in range(NCH):
                    kb.op("vector",
                          lambda e, ch=ch, tlo=tlo:
                          e.tensor_tensor(hn_sb[:, ch, :], shifted_z(ch, tlo),
                                          mr5[:, 0, :], ALU.subtract),
                          waits=[(sC, mrd)], incs=[(sV, 1)])
                    kb.op("vector",
                          lambda e, ch=ch:
                          e.tensor_tensor(hn_sb[:, ch, :], hn_sb[:, ch, :],
                                          mr5[:, 1, :], ALU.mult),
                          waits=[(sV, sV.n)], incs=[(sV, 1)])
                    kb.op("vector",
                          lambda e, ch=ch:
                          e.tensor_scalar(hn_sb[:, ch, :], hn_sb[:, ch, :],
                                          lgb_sb[:, 0, ch:ch + 1], lgb_sb[:, 1, ch:ch + 1],
                                          ALU.mult, ALU.add),
                          waits=[(sV, sV.n), (sW, smalls_w)], incs=[(sV, 1)])
                hnd = sV.n
                for kk in range(NH1):
                    slot = kk % 2
                    for gg in range(2):
                        kb.dma("gpsimd", sWs[slot],
                               w1_sb[:, slot, :].rearrange(
                                   "p (c g m) -> p c g m",
                                   c=NCH, g=2, m=128)[:, :, gg, :],
                               fw_w1[l, :, gg, 128 * kk:128 * kk + 128].rearrange(
                                   "(c p) m -> p c m", p=128),
                               waits=[(sT, w1slot[slot])] if gg == 0 else [])
                    w1ld = (sWs[slot], sWs[slot].n)
                    for gg in range(2):
                        bk = 2 * slot + gg
                        for ch in range(NCH):
                            ww = []
                            if ch == 0:
                                ww = [w1ld, (sV, hnd)]
                                gsem, gval = kb._f5g.get(bk, (sC, 0))
                                ww.append((gsem, gval))
                            kb.op("tensor",
                                  lambda e, gg=gg, ch=ch, bk=bk, slot=slot:
                                  e.matmul(bank(bk),
                                           w1_sb[:, slot, (ch * 2 + gg) * 128:
                                                 (ch * 2 + gg) * 128 + 128],
                                           hn_sb[:, ch, :],
                                           start=(ch == 0), stop=(ch == NCH - 1)),
                                  waits=ww, incs=[(sT, 1)] if ch == NCH - 1 else [])
                    w1slot[slot] = sT.n
                    w1mm = sT.n
                    kb.op("scalar",
                          lambda e, kk=kk, slot=slot:
                          e.activation(glt[:, slot, :], bank(2 * slot + 1), AF.Gelu,
                                       bias=b1_sb[:, 1, kk:kk + 1]),
                          waits=[(sT, w1mm)], incs=[(sC, 1)])
                    kb._f5g[2 * slot + 1] = (sC, sC.n)
                    kb.op("vector",
                          lambda e, kk=kk, slot=slot, tci=tci:
                          e.scalar_tensor_tensor(h_sb[:, kk, TC * tci:TC * tci + TC],
                                                 bank(2 * slot),
                                                 b1_sb[:, 0, kk:kk + 1], glt[:, slot, :],
                                                 ALU.add, ALU.mult),
                          waits=[(sC, sC.n)], incs=[(sV, 1)])
                    kb._f5g[2 * slot] = (sV, sV.n)
            if half == 0:
                kb.tap(f"glu{l}", h_sb, [128, NH1, L // 2], BF16)
            hall = sV.n
            w2bg = [0, 0]
            for dc in range(NCH):
                slot = dc % 2
                kb.dma("gpsimd", sWs[slot],
                       w2_sb[:, slot, :].rearrange("p (k m) -> p k m",
                                                   k=NH1, m=128),
                       fw_w2[l, :, 128 * dc:128 * dc + 128].rearrange(
                           "(k p) m -> p k m", p=128),
                       waits=[(sT, w2slot[slot])])
                w2ld = (sWs[slot], sWs[slot].n)
                for tci in range(2):
                    bk = 4 + 2 * slot + tci
                    for kk in range(NH1):
                        ww = []
                        if kk == 0:
                            ww = [w2ld, (sV, max(hall, w2bg[tci])), (sC, sC.n)]
                        kb.op("tensor",
                              lambda e, slot=slot, kk=kk, bk=bk, tci=tci:
                              e.matmul(bank(bk),
                                       w2_sb[:, slot, 128 * kk:128 * kk + 128],
                                       h_sb[:, kk, TC * tci:TC * tci + TC],
                                       start=(kk == 0), stop=(kk == NH1 - 1)),
                              waits=ww, incs=[(sT, 1)] if kk == NH1 - 1 else [])
                w2slot[slot] = sT.n
                for tci in range(2):
                    bk = 4 + 2 * slot + tci
                    tc = 2 * half + tci
                    kb.op("vector",
                          lambda e, dc=dc, bk=bk, tc=tc:
                          e.scalar_tensor_tensor(z_sb[:, dc, 1 + TC * tc:1 + TC * tc + TC],
                                                 bank(bk), b2_sb[:, dc:dc + 1],
                                                 z_sb[:, dc, 1 + TC * tc:1 + TC * tc + TC],
                                                 ALU.add, ALU.add),
                          waits=[(sT, w2slot[slot])], incs=[(sV, 1)])
                    w2bg[tci] = sV.n
        kb.pop()
        kb.barrier(f"F5-{b}-{l}")
        kb.tap(f"fout{l}", z_sb, [128, NCH, 1 + L], F32)
        kb.pop()


    def phase_OUT(b, z_sb):
        kb.push()
        x1_sb = kb.buf([128, NCH, TC], F32)
        oc = kb.buf([128, NCH, TC], F32)
        sqo = kb.buf([128, TC], F32)
        mro = kb.buf([128, 2, TC], BF16)
        xg = 0
        for tc in range(NTC):
            tsl = slice(TC * tc, TC * tc + TC)
            for ch in range(NCH):
                kb.dma("sync", sX, x1_sb[:, ch, :], x1_park[ch, :, tsl],
                       waits=[(sV, xg)])
            x1w = sX.n
            mrow = row_f[0:1, 0, :]
            rrow = row_f[0:1, 1, :]
            ln_stats(lambda ch: z_sb[:, ch, 1 + TC * tc:1 + TC * tc + TC],
                     EPS1, 1.0, sqo, mrow, rrow, ready=[(sV, sV.n)])
            replicate2(mrow, rrow, mro)
            mrd = sC.n
            for ch in range(NCH):
                kb.op("vector",
                      lambda e, ch=ch, tc=tc:
                      e.tensor_tensor(oc[:, ch, :],
                                      z_sb[:, ch, 1 + TC * tc:1 + TC * tc + TC],
                                      mro[:, 0, :], ALU.subtract),
                      waits=[(sC, mrd), (sX, x1w)], incs=[(sV, 1)])
                kb.op("vector",
                      lambda e, ch=ch:
                      e.tensor_tensor(oc[:, ch, :], oc[:, ch, :],
                                      mro[:, 1, :], ALU.mult),
                      waits=[(sV, sV.n)], incs=[(sV, 1)])
                kb.op("vector",
                      lambda e, ch=ch:
                      e.tensor_tensor(oc[:, ch, :], x1_sb[:, ch, :],
                                      oc[:, ch, :], ALU.add),
                      waits=[(sV, sV.n)], incs=[(sV, 1)])
                kb.dma("sync", sO, outT[b, ch, :, tsl], oc[:, ch, :],
                       waits=[(sV, sV.n)])
            xg = sV.n
        kb.barrier("OUT-done")
        kb.pop()

    kb._f5g = {}
    batches = [0] if (DEV_PHASES is not None and DEV_PHASES != {"ALL"}) else [0, 1]
    if DEV_BATCHES is not None:
        batches = DEV_BATCHES
    full = DEV_PHASES is None or DEV_PHASES == {"ALL"}
    for b in batches:
        kb.push()
        if full or ph_on("A1") or ph_on("A2") or ph_on("A3"):
            Q1_sb = kb.buf([128, NCH, L], BF16, tag=("qkv", b))
            K_sb = kb.buf([128, NCH, L], BF16, tag=("qkv", b))
            V_tm = kb.buf([128, NLT, H * 65], BF16, tag=("qkv", b))
            Ofm = kb.buf([128, NCH, L], BF16, tag=("ofm", b))
            if full or ph_on("A1"):
                phase_A1(b, Q1_sb, K_sb, V_tm)
            if full or ph_on("A2"):
                phase_A2(b, Q1_sb, K_sb, V_tm, Ofm)
            kb.free_tag(("qkv", b))
            if DEV_OUT == "ofm":
                for ch in range(NCH):
                    kb.dma("gpsimd", sO, outT[b, ch, :, :], Ofm[:, ch, :])
                kb.barrier("ofmdump")
                kb.free_tag(("ofm", b))
                kb.pop()
                continue
            if full or ph_on("A3"):
                z_sb = kb.buf([128, NCH, 1 + L], F32, tag=("z", b))
                phase_A3(b, Ofm, z_sb)
            kb.free_tag(("ofm", b))
            if DEV_OUT == "z":
                for ch in range(NCH):
                    kb.dma("sync", sO, outT[b, ch, :, :], z_sb[:, ch, 1:])
                kb.barrier("zdump")
                kb.free_tag(("z", b))
                kb.pop()
                continue
        if full or ph_on("F"):
            if not (full or ph_on("A3")):
                z_sb = kb.buf([128, NCH, 1 + L], F32, tag=("z", b))
            rotm_sb = kb.buf([128, 128], BF16, tag=("ftab", b))
            bd8_sb = kb.buf([128, 4, 8], BF16, tag=("ftab", b))
            oneh_sb = kb.buf([8, 4, 128], F32, tag=("ftab", b))
            kb.dma("gpsimd", sW, rotm_sb, c_rotm[:])
            for c4 in range(4):
                kb.dma("gpsimd", sW, bd8_sb[:, c4, :], c_bd8[c4])
                kb.dma("sync", sW, oneh_sb[0:8, c4, :], c_oneh[c4])
            for l in range(2):
                kb.push()
                phase_F(b, l, z_sb, rotm_sb, bd8_sb, oneh_sb)
                kb.pop()
            kb.free_tag(("ftab", b))
            if full or ph_on("OUT"):
                phase_OUT(b, z_sb)
        kb.free_tag(("z", b))
        kb.pop()

    kb.barrier("end")
    kb.op("sync", None, [(sO, sO.n), (kb.tap_sem, kb.tap_sem.n)], [])
    return kb, {"outT": outT}


def _host_prep(inputs):
    c = host_constants()
    x = np.ascontiguousarray(inputs["L0_attn_x"])
    base = {
        "Wq": inputs["L0_attn_Wq"], "Wk": inputs["L0_attn_Wk"],
        "Wv": inputs["L0_attn_Wv"], "Wr": inputs["L0_attn_Wr"],
        "Wo": inputs["L0_attn_Wo"],
        "gamma8": np.asarray(inputs["L0_attn_gamma"]).reshape(NCH, 128),
        "beta8": np.asarray(inputs["L0_attn_beta"]).reshape(NCH, 128),
        "bo8": np.asarray(inputs["L0_attn_bo"]).reshape(NCH, 128),
        "ub8": np.asarray(inputs["L0_attn_u_bias"]).reshape(NCH, 128),
        "vb8": np.asarray(inputs["L0_attn_v_bias"]).reshape(NCH, 128),
        "rel": np.ascontiguousarray(np.asarray(inputs["L0_attn_rel_pos_emb"])[2953:7048]),
        "fw_qkv": inputs["L0_ffn_wqkv"],
        "fw_wo": inputs["L0_ffn_wo"],
        "f_qs": np.tile(np.asarray(inputs["L0_ffn_q_scale"]), (1, 2)),
        "f_ks": np.tile(np.asarray(inputs["L0_ffn_k_scale"]), (1, 2)),
        "f_lng": np.asarray(inputs["L0_ffn_ln_g"]).reshape(2, NCH, 128),
        "f_lnb": np.asarray(inputs["L0_ffn_ln_b"]).reshape(2, NCH, 128),
        "fw_b2": np.asarray(inputs["L0_ffn_b2"]).reshape(2, NCH, 128),
        "c_ident": c["ident"], "c_ones1": c["ones1"],
        "c_rotqc": c["rotqc"], "c_rotqs": c["rotqs"],
        "c_rotkc": c["rotkc"], "c_rotks": c["rotks"],
        "c_rotm": c["rotm"], "c_masks": c["masks"], "c_fpool": c["fpool"],
        "c_bd8": c["bd8"], "c_oneh": c["oneh"],
    }
    w1 = np.asarray(inputs["L0_ffn_w1"])
    b1 = np.asarray(inputs["L0_ffn_b1"])
    w2 = np.asarray(inputs["L0_ffn_w2"])
    w1p = np.zeros((2, D, 2, HID_PAD), np.float32)
    w1p[:, :, 0, :2730] = w1[:, :, :2730]
    w1p[:, :, 1, :2730] = w1[:, :, 2730:]
    b1p = np.zeros((2, 2, HID_PAD), np.float32)
    b1p[:, 0, :2730] = b1[:, :2730]
    b1p[:, 1, :2730] = b1[:, 2730:]
    w2p = np.zeros((2, HID_PAD, D), np.float32)
    w2p[:, :2730, :] = w2
    base["fw_w1"] = w1p
    base["fw_b1"] = b1p.reshape(2, 2, NH1, 128)
    base["fw_w2"] = w2p
    base = {k: np.ascontiguousarray(v, dtype=np.float32) for k, v in base.items()}

    cores = DEV_CORES if DEV_CORES else list(range(N_CORES))
    per_core = []
    for core in cores:
        m = dict(base)
        xc = x[2 * core: 2 * core + 2]
        m["xT"] = np.ascontiguousarray(xc.transpose(0, 2, 1).reshape(2, NCH, 128, L))
        per_core.append(m)
    return per_core, cores


def kernel(**inputs):
    per_core, cores = _host_prep(inputs)
    kb, info = build_program()
    kb.build()
    res = run_bass_kernel_spmd(kb.nc, per_core, core_ids=cores)
    kernel.last_results = res.results
    outs = []
    for r in res.results:
        o = np.asarray(r["outT"]).reshape(2, D, L).transpose(0, 2, 1)
        outs.append(o)
    return np.concatenate(outs, axis=0).astype(np.float32)
